# version 08
# baseline (speedup 1.0000x reference)
import contextlib
import math
import numpy as np
import ml_dtypes
import concourse.bass as bass
import concourse.mybir as mybir
from concourse.bass_utils import run_bass_kernel_spmd

F32 = mybir.dt.float32
I32 = mybir.dt.int32
BF16 = mybir.dt.bfloat16
U8 = mybir.dt.uint8
ALU = mybir.AluOpType
AF = mybir.ActivationFunctionType
AX = mybir.AxisListType
BF = ml_dtypes.bfloat16

NCORES = 8
D = 4096
DQ = 4096
DKV = 1024
HD = 128
NKV = 8
GQ = 4
FF = 11008
FSH = FF // NCORES
TOK = 1024
NB = 4
S = 2048
R = 64
SEL = 128
QBLK = 16
BIGM = 30000.0
SCALE = float(128.0 ** -0.5)

SLOT_A = [0, 3, 4, 7, 8, 11, 12, 15]
SLOT_B = [1, 2, 5, 6, 9, 10, 13, 14]
EXT = [2, 4, 6, 8, 10, 12, 14, 16]


def _row_perm():
    rows = []
    for b in range(NB):
        for slots in (SLOT_A, SLOT_B):
            for t in slots:
                base = 2048 * b + 128 * t
                rows.extend(range(base, base + 128))
    return np.asarray(rows, dtype=np.int64)



ENGINES = ("sync", "gpsimd", "vector", "scalar", "tensor")


class Prog:

    def __init__(self):
        self.ops = {e: [] for e in ENGINES}
        self.cnt = {}

    def bump(self, sem, by):
        self.cnt[sem] = self.cnt.get(sem, 0) + by
        return self.cnt[sem]

    def val(self, sem):
        return self.cnt.get(sem, 0)

    def dma(self, eng, out, in_, sem, **kw):
        tot = self.bump(sem, 16)

        def fn(e, _o=out, _i=in_, _s=sem, _k=kw):
            e.dma_start(out=_o, in_=_i, **_k).then_inc(self.sems[_s], 16)
        self.ops[eng].append(fn)
        return tot

    def wait(self, eng, sem, val):
        def fn(e, _s=sem, _v=val):
            e.wait_ge(self.sems[_s], _v)
        self.ops[eng].append(fn)

    CHAIN = {"vector": "vc", "scalar": "sc", "gpsimd": "gpc"}

    def op(self, eng, fn, sem=None):
        ch = self.CHAIN.get(eng)
        if ch is not None:
            self.wait(eng, ch, self.val(ch))
            tot = self.bump(ch, 1)

            def g(e, _f=fn, _c=ch):
                _f(e).then_inc(self.sems[_c], 1)
            self.ops[eng].append(g)
            return tot
        if sem is None:
            self.ops[eng].append(lambda e, _f=fn: _f(e))
            return None
        tot = self.bump(sem, 1)

        def g(e, _f=fn, _s=sem):
            _f(e).then_inc(self.sems[_s], 1)
        self.ops[eng].append(g)
        return tot

    def phase_barrier(self, engines=ENGINES):
        targets = [(n, v) for n, v in sorted(self.cnt.items()) if v]
        for eng in engines:
            for n, v in targets:
                self.wait(eng, n, v)

    def barrier(self, producer_eng, sem, waiters):
        tot = self.op(producer_eng, lambda e: e.engine_nop()
                      if producer_eng in ("vector", "scalar") else e.nop(), sem)
        for w in waiters:
            self.wait(w, sem, tot)
        return tot

    def replay(self, block, sems):
        self.sems = sems

        def mk(eng):
            def run(e):
                for fn in self.ops[eng]:
                    fn(e)
            return run
        block.sync(mk("sync"))
        block.gpsimd(mk("gpsimd"))
        block.vector(mk("vector"))
        block.scalar(mk("scalar"))
        block.tensor(mk("tensor"))


class Sbuf:
    BASE = 16544
    LIMIT = 229376

    def __init__(self, nc):
        self.nc = nc
        self.off = self.BASE

    def alloc(self, name, shape, dtype):
        nbytes = int(np.prod(shape[1:])) * mybir.dt.size(dtype)
        off = (self.off + 31) // 32 * 32
        assert off + nbytes <= self.LIMIT, (
            f"SBUF overflow: {name} at {off}+{nbytes} > {self.LIMIT}")
        t = self.nc.alloc_sbuf_tensor_at(name, list(shape), dtype, offset=off)
        self.off = off + nbytes
        return t


def _nop(eng_name):
    def fn(e):
        return e.engine_nop() if hasattr(e, "engine_nop") else e.nop()
    return fn



def emit_qdq(p, scr, v_ap, out_ap, P, F, SC="sc", scale_ap=None,
             use_gpsimd=False):
    A, Bs, Cs, Es, Ii, M0, M1, M2, SG = (
        scr["A"], scr["B"], scr["C"], scr["E"], scr["I"],
        scr["M0"], scr["M1"], scr["M2"], scr["SG"])
    nb = F // QBLK
    v3 = v_ap.rearrange("p (b c) -> p b c", c=QBLK)

    p.op("scalar", lambda e: e.activation(SG[:P, :F], v_ap, AF.Sign))
    sgv = p.op("scalar", lambda e: e.activation(Es[:P, :F], v_ap, AF.Abs), SC)

    p.op("vector", lambda e: e.tensor_reduce(
        M0[:P, :nb].unsqueeze(2), v3, AX.X, ALU.max, apply_absolute_value=True))
    p.op("vector", lambda e: e.tensor_scalar(
        M0[:P, :nb], M0[:P, :nb], 1e-30, None, ALU.max))
    p.op("vector", lambda e: e.reciprocal(M1[:P, :nb], M0[:P, :nb]))
    p.op("vector", lambda e: e.tensor_scalar(
        M1[:P, :nb], M1[:P, :nb], 6.0, None, ALU.mult))
    p.op("vector", lambda e: e.tensor_scalar(
        M2[:P, :nb], M0[:P, :nb], 1.0 / 6.0, None, ALU.mult))
    m1b = M1[:P, :nb].unsqueeze(2).broadcast_to([P, nb, QBLK])
    m2b = M2[:P, :nb].unsqueeze(2).broadcast_to([P, nb, QBLK])

    p.wait("vector", SC, sgv)
    tv = p.op("vector", lambda e: e.tensor_tensor(
        Bs[:P, :F].rearrange("p (b c) -> p b c", c=QBLK),
        Es[:P, :F].rearrange("p (b c) -> p b c", c=QBLK), m1b, ALU.mult))

    geng = "gpsimd" if use_gpsimd else "vector"
    gch = "gpc" if use_gpsimd else "vc"
    If = Ii[:P, :F].bitcast(F32)
    p.op("vector", lambda e: e.tensor_scalar(
        Cs[:P, :F], Bs[:P, :F], 0.25, None, ALU.is_ge))
    for b in (0.75, 1.25, 1.75):
        p.op("vector", lambda e, _b=b: e.scalar_tensor_tensor(
            Cs[:P, :F], Bs[:P, :F], _b, Cs[:P, :F], ALU.is_ge, ALU.add))
    if use_gpsimd:
        p.wait(geng, "vc", tv)
    p.op(geng, lambda e: e.tensor_scalar(
        Es[:P, :F], Bs[:P, :F], 2.5, None, ALU.is_ge))
    p.op(geng, lambda e: e.scalar_tensor_tensor(
        Es[:P, :F], Bs[:P, :F], 3.5, Es[:P, :F], ALU.is_ge, ALU.add))
    gv = p.op(geng, lambda e: e.tensor_scalar(
        If, Bs[:P, :F], 5.0, 2.0, ALU.is_ge, ALU.mult))
    if use_gpsimd:
        p.wait("vector", gch, gv)
    p.op("vector", lambda e: e.scalar_tensor_tensor(
        Cs[:P, :F], Cs[:P, :F], 0.5, Es[:P, :F], ALU.mult, ALU.add))
    p.op("vector", lambda e: e.tensor_tensor(
        Cs[:P, :F], Cs[:P, :F], If, ALU.add))
    p.op("vector", lambda e: e.tensor_tensor(
        Bs[:P, :F].rearrange("p (b c) -> p b c", c=QBLK),
        SG[:P, :F].rearrange("p (b c) -> p b c", c=QBLK), m2b, ALU.mult))
    if scale_ap is not None:
        p.op("vector", lambda e: e.scalar_tensor_tensor(
            Cs[:P, :F], Cs[:P, :F], scale_ap, Bs[:P, :F], ALU.mult, ALU.mult))
        vcv = p.op("vector", lambda e: e.tensor_copy(out_ap, Cs[:P, :F]))
    else:
        vcv = p.op("vector", lambda e: e.tensor_tensor(
            out_ap, Cs[:P, :F], Bs[:P, :F], ALU.mult))
    return vcv



def build_launch1():
    nc = bass.Bass("TRN2", target_bir_lowering=False)
    sb = Sbuf(nc)
    p = Prog()

    xp = nc.dram_tensor("xp", [TOK, D], F32, kind="ExternalInput").ap()
    wq = nc.dram_tensor("wq", [D, DQ], F32, kind="ExternalInput").ap()
    wk = nc.dram_tensor("wk", [D, DKV], F32, kind="ExternalInput").ap()
    wv = nc.dram_tensor("wv", [D, DKV], F32, kind="ExternalInput").ap()
    la = nc.dram_tensor("la", [D, R], F32, kind="ExternalInput").ap()
    wgs = nc.dram_tensor("wgs", [FSH, D], F32, kind="ExternalInput").ap()
    wus = nc.dram_tensor("wus", [FSH, D], F32, kind="ExternalInput").ap()
    wds = nc.dram_tensor("wds", [D, FSH], F32, kind="ExternalInput").ap()
    wdm = nc.dram_tensor("wdm", [128, FSH], U8, kind="ExternalInput").ap()
    osc = nc.dram_tensor("osc", [128, D // 128], F32, kind="ExternalInput").ap()
    idn = nc.dram_tensor("idn", [128, 128], BF16, kind="ExternalInput").ap()

    xq_o = nc.dram_tensor("xq_o", [TOK, D], BF16, kind="ExternalOutput").ap()
    qt_o = nc.dram_tensor("qt_o", [DQ, TOK], BF16, kind="ExternalOutput").ap()
    kt_o = nc.dram_tensor("kt_o", [DKV, TOK], BF16, kind="ExternalOutput").ap()
    v_o = nc.dram_tensor("v_o", [TOK, DKV], BF16, kind="ExternalOutput").ap()
    o1t_o = nc.dram_tensor("o1t_o", [R, TOK], BF16, kind="ExternalOutput").ap()
    wg_o = nc.dram_tensor("wg_o", [FSH, D], BF16, kind="ExternalOutput").ap()
    wu_o = nc.dram_tensor("wu_o", [FSH, D], BF16, kind="ExternalOutput").ap()
    wd_o = nc.dram_tensor("wd_o", [D, FSH], BF16, kind="ExternalOutput").ap()

    xn_d = nc.dram_tensor("xn_d", [TOK, D], BF16).ap()

    A = sb.alloc("A", [128, D], F32)
    Bt = sb.alloc("B", [128, D], F32)
    Ct = sb.alloc("C", [128, D], F32)
    Et = sb.alloc("E", [128, D], F32)
    It = sb.alloc("I", [128, D], I32)
    SG = sb.alloc("SG", [128, D], F32)
    XA = [sb.alloc(f"XA{i}", [128, D], F32) for i in range(2)]
    XSQ = sb.alloc("XSQ", [128, D], F32)
    assert sb.off <= Sbuf.BASE + 160 * 1024
    XNT = nc.alloc_sbuf_tensor_at("XNT", [128, 32 * TOK], BF16, offset=Sbuf.BASE)
    WB = [nc.alloc_sbuf_tensor_at(f"WB{i}", [128, 32 * 128], BF16,
                                  offset=Sbuf.BASE + 65536 + i * 8192) for i in range(2)]
    QSTG = [nc.alloc_sbuf_tensor_at(f"QSTG{i}", [128, TOK], BF16,
                                    offset=Sbuf.BASE + 81920 + 2048 * i)
            for i in range(2)]
    VSTG = [nc.alloc_sbuf_tensor_at(f"VSTG{i}", [128, TOK], BF16,
                                    offset=Sbuf.BASE + 86016 + 2048 * i)
            for i in range(2)]
    LAB = nc.alloc_sbuf_tensor_at("LAB", [128, 32 * R], BF16, offset=Sbuf.BASE + 90112)
    O1S = nc.alloc_sbuf_tensor_at("O1S", [64, TOK], BF16, offset=Sbuf.BASE + 94208)
    sb.off = Sbuf.BASE + 160 * 1024
    M0 = sb.alloc("M0", [128, D // QBLK], F32)
    M1 = sb.alloc("M1", [128, D // QBLK], F32)
    M2 = sb.alloc("M2", [128, D // QBLK], F32)
    ST1 = sb.alloc("ST1", [128, 1], F32)
    ST2 = sb.alloc("ST2", [128, 1], F32)
    TMP1 = sb.alloc("TMP1", [128, 1], F32)
    RSTD = sb.alloc("RSTD", [128, 1], F32)
    NBIAS = sb.alloc("NBIAS", [128, 1], F32)
    XNB = [sb.alloc(f"XNB{i}", [128, D], BF16) for i in range(2)]
    XQB = [sb.alloc(f"XQB{i}", [128, D], BF16) for i in range(2)]
    WMSK = sb.alloc("WMSK", [128, FSH], U8)
    OSC = sb.alloc("OSC", [128, D // 128], F32)
    RAWB = sb.alloc("RAWB", [128, FSH], F32)
    IDN = sb.alloc("IDN", [128, 128], BF16)

    scr = {"A": A, "B": Bt, "C": Ct, "E": Et, "I": It,
           "M0": M0, "M1": M1, "M2": M2, "SG": SG}

    ps = [nc.alloc_psum_tensor(f"ps{i}", [128, 512], F32) for i in range(4)]
    pst_t = nc.alloc_psum_tensor("pst_t", [128, 128], BF16)

    LD, STs, VC, SC, MM, GP = "ld", "st", "vc", "sc", "mm", "gp"

    p.dma("sync", WMSK[:], wdm, LD)
    p.dma("sync", OSC[:], osc, LD)
    p.dma("sync", IDN[:], idn, LD)
    const_ld = p.val(LD)

    tiles = []
    for i in range(TOK // 128):
        tiles.append(("x", i))
    for tag in ("wg", "wu"):
        for r0 in range(0, FSH, 128):
            tiles.append((tag, r0))
    for r0 in range(0, D, 128):
        tiles.append(("wd", r0))

    ld_sem = ["lda", "ldb"]
    st_sem = ["sta", "stb"]
    ld_val = {}
    st_val = {}
    qdq_val = {}

    for ti, (kind, idx) in enumerate(tiles):
        par = ti % 2
        xa = XA[par]
        xnb = XNB[par]
        if ti >= 2:
            p.wait("sync", "vc", qdq_val[ti - 2])
        if kind == "x":
            src_ap = xp[128 * idx:128 * (idx + 1), :]
            rows, Fw = 128, D
        elif kind == "wg":
            rows = min(128, FSH - idx)
            src_ap = wgs[idx:idx + rows, :]
            Fw = D
        elif kind == "wu":
            rows = min(128, FSH - idx)
            src_ap = wus[idx:idx + rows, :]
            Fw = D
        else:
            rows, Fw = 128, FSH
            src_ap = wds[idx:idx + rows, :]
        ld_val[ti] = p.dma("sync", xa[:rows, :Fw], src_ap, ld_sem[par])
        p.wait("vector", ld_sem[par], ld_val[ti])
        p.wait("scalar", ld_sem[par], ld_val[ti])
        if ti == 0:
            p.wait("vector", LD, const_ld)
            p.wait("scalar", LD, const_ld)

        if kind == "x":
            p.op("vector", lambda e, _x=xa: e.reduce_sum(ST1[:], _x[:], AX.X))
            scv = p.op("scalar", lambda e, _x=xa: e.activation(
                XSQ[:], _x[:], AF.Square), SC)
            p.wait("vector", SC, scv)
            p.op("vector", lambda e: e.reduce_sum(ST2[:], XSQ[:], AX.X))
            p.op("vector", lambda e: e.tensor_scalar(
                ST1[:], ST1[:], 1.0 / D, None, ALU.mult))
            p.op("vector", lambda e: e.tensor_scalar(
                ST2[:], ST2[:], 1.0 / D, None, ALU.mult))
            p.op("vector", lambda e: e.tensor_tensor(
                TMP1[:], ST1[:], ST1[:], ALU.mult))
            p.op("vector", lambda e: e.tensor_tensor(
                ST2[:], ST2[:], TMP1[:], ALU.subtract))
            vcv = p.op("vector", lambda e: e.tensor_scalar(
                ST2[:], ST2[:], 1e-5, None, ALU.add))
            p.wait("scalar", "vc", vcv)
            p.op("scalar", lambda e: e.activation(RSTD[:], ST2[:], AF.Ln))
            scv = p.op("scalar", lambda e: e.activation(
                RSTD[:], RSTD[:], AF.Exp, scale=-0.5), SC)
            p.wait("vector", SC, scv)
            p.op("vector", lambda e: e.tensor_tensor(
                NBIAS[:], ST1[:], RSTD[:], ALU.mult))
            p.op("vector", lambda e: e.tensor_scalar(
                NBIAS[:], NBIAS[:], -1.0, None, ALU.mult))
            p.op("vector", lambda e, _x=xa: e.tensor_scalar(
                A[:], _x[:], RSTD[:], NBIAS[:], ALU.mult, ALU.add))
            if ti >= 2:
                p.wait("vector", st_sem[par], st_val[ti - 2])
            vcv = p.op("vector", lambda e, _n=xnb: e.tensor_copy(_n[:], A[:]))
            p.wait("scalar", "vc", vcv)
            emit_qdq(p, scr, A[:], XQB[par][:], 128, D, SC)
            vcv = p.op("vector", lambda e, _n=xnb, _q=XQB[par]: e.tensor_copy(
                _q[:, D - SEL:D], _n[:, D - SEL:D]))
            qdq_val[ti] = vcv
            p.wait("sync", "vc", vcv)
            p.dma("sync", xn_d[128 * idx:128 * (idx + 1), :], xnb[:],
                  st_sem[par])
            st_val[ti] = p.dma("sync", xq_o[128 * idx:128 * (idx + 1), :],
                               XQB[par][:], st_sem[par])
        else:
            scaled = kind == "wd"
            sc_ap = OSC[:rows, idx // 128:idx // 128 + 1] if scaled else None
            if ti >= 2:
                p.wait("vector", st_sem[par], st_val[ti - 2])
            vcv = emit_qdq(p, scr, xa[:rows, :Fw], xnb[:rows, :Fw], rows, Fw,
                           SC, scale_ap=sc_ap)
            if kind in ("wg", "wu"):
                vcv = p.op("vector", lambda e, _r=rows, _x=xa, _n=xnb:
                           e.tensor_copy(_n[:_r, D - SEL:D],
                                         _x[:_r, D - SEL:D]))
            else:
                p.op("vector", lambda e, _r=rows, _x=xa, _s=sc_ap:
                     e.tensor_scalar(RAWB[:_r, :FSH], _x[:_r, :FSH], _s,
                                     None, ALU.mult))
                vcv = p.op("vector", lambda e, _r=rows, _n=xnb:
                           e.copy_predicated(_n[:_r, :FSH], WMSK[:_r, :FSH],
                                             RAWB[:_r, :FSH]))
            qdq_val[ti] = vcv
            p.wait("sync", "vc", vcv)
            dst = {"wg": wg_o, "wu": wu_o, "wd": wd_o}[kind]
            st_val[ti] = p.dma("sync", dst[idx:idx + rows, :],
                               xnb[:rows, :Fw], st_sem[par])

    p.phase_barrier()

    for t in range(32):
        p.dma("sync", XNT[:, TOK * t:TOK * (t + 1)],
              xn_d[:, 128 * t:128 * (t + 1)], LD, transpose=True)
    xnt_ld = p.val(LD)
    p.wait("tensor", LD, xnt_ld)

    def xnt(t):
        return XNT[:, TOK * t:TOK * (t + 1)]

    lab_ld = p.dma("gpsimd", LAB[:].rearrange("p (t r) -> p t r", r=R),
                   la.rearrange("(t p) r -> p t r", p=128), "lab")

    blocks = []
    for o in range(32):
        blocks.append(("q", o))
    for o in range(8):
        blocks.append(("k", o))
    for o in range(8):
        blocks.append(("v", o))
    wsrc_map = {"q": wq, "k": wk, "v": wv}
    dst_map = {"q": qt_o, "k": kt_o, "v": v_o}
    gp_sem = ["gpa", "gpb"]
    gld = {}
    mm_done = {}
    ev_done = {}

    def load_block(bi):
        kindb, o = blocks[bi]
        par = bi % 2
        if bi >= 2:
            p.wait("gpsimd", MM, mm_done[bi - 2])
        gld[bi] = p.dma("gpsimd",
                        WB[par][:].rearrange("p (t c) -> p t c", c=128),
                        wsrc_map[kindb][:, 128 * o:128 * (o + 1)].rearrange(
                            "(t p) c -> p t c", p=128), gp_sem[par])

    load_block(0)
    load_block(1)
    for bi in range(len(blocks)):
        kindb, o = blocks[bi]
        par = bi % 2
        p.wait("tensor", gp_sem[par], gld[bi])
        if bi >= 2:
            p.wait("tensor", "vc", ev_done[bi - 2])
        for nchunk in range(2):
            pb = ps[2 * par + nchunk]
            for dt in range(32):
                mmv = p.op("tensor",
                           lambda e, _b=par, _dt=dt, _n=nchunk, _p=pb: e.matmul(
                               _p[:, 0:512],
                               WB[_b][:, 128 * _dt:128 * (_dt + 1)],
                               xnt(_dt)[:, 512 * _n:512 * (_n + 1)],
                               start=(_dt == 0), stop=(_dt == 31)),
                           MM if dt == 31 else None)
        mm_done[bi] = mmv
        if bi + 2 < len(blocks):
            load_block(bi + 2)
        p.wait("vector", MM, mmv)
        if bi >= 2:
            p.wait("vector", st_sem[par], st_val.get(("p2", bi - 2), 0))
        for nchunk in range(2):
            pb = ps[2 * par + nchunk]
            vcv = p.op("vector", lambda e, _p=pb, _n=nchunk, _q=par:
                       e.tensor_copy(QSTG[_q][:, 512 * _n:512 * (_n + 1)],
                                     _p[:, 0:512]))
        ev_done[bi] = vcv
        if kindb != "v":
            p.wait("sync", "vc", vcv)
            st_val[("p2", bi)] = p.dma(
                "sync", dst_map[kindb][128 * o:128 * (o + 1), :],
                QSTG[par][:], st_sem[par])
        else:
            p.wait("tensor", "vc", ev_done[bi])
            for tt in range(8):
                mmv2 = p.op("tensor", lambda e, _tt=tt, _q=par: e.transpose(
                    pst_t[:], QSTG[_q][:, 128 * _tt:128 * (_tt + 1)], IDN[:]),
                    MM)
                p.wait("vector", MM, mmv2)
                vcv = p.op("vector", lambda e, _tt=tt, _q=par: e.tensor_copy(
                    VSTG[_q][:, 128 * _tt:128 * (_tt + 1)], pst_t[:]))
                p.wait("tensor", "vc", vcv)
            p.wait("sync", "vc", vcv)
            for tt in range(8):
                st_val[("p2", bi)] = p.dma(
                    "sync", dst_map[kindb][128 * tt:128 * (tt + 1),
                                           128 * o:128 * (o + 1)],
                    VSTG[par][:, 128 * tt:128 * (tt + 1)], st_sem[par])
            mm_done[bi] = mmv2

    p.wait("tensor", "lab", lab_ld)
    p.wait("tensor", "vc", ev_done[len(blocks) - 1])
    for nchunk in range(2):
        pb = ps[2 + nchunk]
        for dt in range(32):
            mmv = p.op("tensor", lambda e, _dt=dt, _n=nchunk, _p=pb: e.matmul(
                _p[0:64, 0:512], LAB[:, R * _dt:R * (_dt + 1)],
                xnt(_dt)[:, 512 * _n:512 * (_n + 1)],
                start=(_dt == 0), stop=(_dt == 31)),
                MM if dt == 31 else None)
        p.wait("vector", MM, mmv)
        vcv = p.op("vector", lambda e, _p=pb, _n=nchunk: e.tensor_copy(
            O1S[0:64, 512 * _n:512 * (_n + 1)], _p[0:64, 0:512]))
    p.wait("sync", "vc", vcv)
    p.dma("sync", o1t_o, O1S[0:64, :], STs)

    p.phase_barrier(("sync",))

    sems = {}
    stack = contextlib.ExitStack()
    for name in (LD, STs, VC, SC, MM, GP, "gpc", "lda", "ldb",
                 "sta", "stb", "gpa", "gpb", "lab"):
        sems[name] = stack.enter_context(nc.semaphore(name))
    with nc.Block() as block:
        p.replay(block, sems)
    stack.close()
    return nc



def _launch1_inputs(inputs):
    X = np.asarray(inputs["L0_attn_hidden_states"]).reshape(NB * S, D)
    up = np.asarray(inputs["L0_ffn_up_reorder_index"]).astype(np.int64)
    dn = np.asarray(inputs["L0_ffn_down_reorder_index"]).astype(np.int64)
    PR = _row_perm()

    Xp = np.ascontiguousarray(X[PR][:, up])
    WQ = np.ascontiguousarray(np.asarray(inputs["L0_attn_w_q"])[up, :])
    WK = np.ascontiguousarray(np.asarray(inputs["L0_attn_w_k"])[up, :])
    WV = np.ascontiguousarray(np.asarray(inputs["L0_attn_w_v"])[up, :])
    LA = np.ascontiguousarray(np.asarray(inputs["L0_ffn_lr_A"])[up, :])
    WGp = np.asarray(inputs["L0_ffn_w_gate"])[dn][:, up]
    WUp = np.asarray(inputs["L0_ffn_w_up"])[dn][:, up]
    WDp = np.asarray(inputs["L0_ffn_w_down"])[:, dn]
    oscale = np.asarray(inputs["L0_ffn_mlp_output_scale"]).astype(np.float32)
    osc = np.ascontiguousarray(oscale.reshape(D // 128, 128).T)
    idn = np.eye(128, dtype=BF)

    in_maps = []
    for c in range(NCORES):
        f0 = FSH * c
        keep = (np.arange(f0, f0 + FSH) >= FF - SEL).astype(np.uint8)
        wdmask = np.ascontiguousarray(np.broadcast_to(keep, (128, FSH)))
        in_maps.append({
            "xp": np.ascontiguousarray(Xp[TOK * c:TOK * (c + 1)]),
            "wq": WQ, "wk": WK, "wv": WV, "la": LA,
            "wgs": np.ascontiguousarray(WGp[f0:f0 + FSH]),
            "wus": np.ascontiguousarray(WUp[f0:f0 + FSH]),
            "wds": np.ascontiguousarray(WDp[:, f0:f0 + FSH]),
            "wdm": wdmask, "osc": osc, "idn": idn,
        })
    return in_maps, PR



FCH = 512
NFCH = 22
CW = [512] * 21 + [256]
FT = FF // 128
FHALF = FT // 2


def build_launch2():
    nc = bass.Bass("TRN2", target_bir_lowering=False)
    sb = Sbuf(nc)
    p = Prog()

    xres = nc.dram_tensor("xres", [TOK, D], F32, kind="ExternalInput").ap()
    xqt = nc.dram_tensor("xqt", [128, 32 * TOK], BF16,
                         kind="ExternalInput").ap()
    qt = nc.dram_tensor("qt", [DQ, TOK], BF16, kind="ExternalInput").ap()
    ktb = nc.dram_tensor("ktb", [DKV, S], BF16, kind="ExternalInput").ap()
    vaug = nc.dram_tensor("vaug", [S, NKV * 129], BF16, kind="ExternalInput").ap()
    o1t = nc.dram_tensor("o1t", [R, TOK], BF16, kind="ExternalInput").ap()
    lrb = nc.dram_tensor("lrb", [R, D], F32, kind="ExternalInput").ap()
    wo = nc.dram_tensor("wo", [8 * 128, 32 * 512], F32,
                        kind="ExternalInput").ap()
    wgt = nc.dram_tensor("wgt", [NFCH * 128, 32 * FCH], BF16,
                         kind="ExternalInput").ap()
    wut = nc.dram_tensor("wut", [NFCH * 128, 32 * FCH], BF16,
                         kind="ExternalInput").ap()
    wdt = nc.dram_tensor("wdt", [16 * 128, FHALF * 512], BF16,
                         kind="ExternalInput").ap()
    mskr = nc.dram_tensor("mskr", [8, 2, 128, 128], BF16, kind="ExternalInput").ap()
    ucst = nc.dram_tensor("ucst", [128, 128], BF16, kind="ExternalInput").ap()
    one1 = nc.dram_tensor("one1", [1, 128], BF16, kind="ExternalInput").ap()
    onec = nc.dram_tensor("onec", [128, 1], BF16, kind="ExternalInput").ap()
    idn = nc.dram_tensor("idn", [128, 128], BF16, kind="ExternalInput").ap()

    out = nc.dram_tensor("out", [TOK, D], F32, kind="ExternalOutput").ap()

    attnt_d = nc.dram_tensor("attnt_d", [128, 32 * TOK], BF16).ap()
    hqt_d = nc.dram_tensor("hqt_d", [FF, TOK], BF16).ap()
    down_d = nc.dram_tensor("down_d", [TOK, D], F32).ap()

    OV = Sbuf.BASE

    def ov(name, shape, dtype, off):
        return nc.alloc_sbuf_tensor_at(name, list(shape), dtype, offset=OV + off)

    KTS = ov("KTS", [128, NKV * S], BF16, 0)
    VAS = ov("VAS", [128, 16 * 1032], BF16, 32768)
    QTS = ov("QTS", [128, 32 * TOK], BF16, 65792)
    KSQ = ov("KSQ", [128, S], BF16, 131328)
    PT = [ov(f"PT{i}", [128, 512], BF16, 135424 + 1024 * i) for i in range(2)]
    ATL = ov("ATL", [128, 128], BF16, 137472)
    ATLT = ov("ATLT", [128, 128], BF16, 137728)
    MSK = ov("MSK", [128, 8 * 2 * 128], BF16, 137984)
    XQS = ov("XQS", [128, 32 * TOK], BF16, 0)
    WGC = [ov(f"WGC{i}", [128, 32 * FCH], BF16, 65536 + 32768 * i)
           for i in range(2)]
    WUC = ov("WUC", [128, 32 * FCH], BF16, 131072)
    QS = {}
    boff = 163840
    for nm, dt_ in (("A", F32), ("B", F32), ("C", F32), ("E", F32),
                    ("I", I32), ("SG", F32)):
        QS[nm] = ov("q" + nm, [128, FCH], dt_, boff)
        boff += FCH * 4
    H_ = ov("H_", [128, FCH], F32, boff); boff += FCH * 4
    SILB = ov("SILB", [128, 8 * FCH], F32, boff); boff += 8 * FCH * 4
    HQB = [None] * 4
    for _ti in range(4):
        HQB[_ti] = ov(f"HQ{_ti}", [128, FCH], BF16, boff); boff += FCH * 2
    HQTW = [None] * 4
    for _ji in range(4):
        HQTW[_ji] = ov(f"HQW{_ji}", [128, TOK], BF16, boff)
        boff += TOK * 2
    WDC = [ov("WDCa", [128, FHALF * 512], BF16, 0),
           ov("WDCb", [128, FHALF * 512], BF16, 69888)]
    HQF = [ov(f"HQF{i}", [128, TOK], BF16, 45056 + 2048 * i)
           for i in range(4)]
    DSTG = [ov(f"DSTG{i}", [128, 512], F32, 53248 + 2048 * i)
            for i in range(2)]
    ATT2F = ov("ATT2F", [128, 32 * TOK], BF16, 113920)
    WOC = [ov(f"WOC{i}", [128, 32 * 512], BF16, 32768 * i)
           for i in range(2)]
    LRB = ov("LRB", [64, D], BF16, 69888)
    DT2 = [ov(f"DT{i}", [128, 512], F32, 78080 + 2048 * i) for i in range(2)]
    XT2 = [ov(f"XT{i}", [128, 512], F32, 82176 + 2048 * i) for i in range(2)]
    OS2 = [ov(f"OS{i}", [128, 512], F32, 86272 + 2048 * i) for i in range(2)]

    sb.off = Sbuf.BASE + 206848
    UC = sb.alloc("UC", [128, 128], BF16)
    IDN = sb.alloc("IDN", [128, 128], BF16)
    O1S = sb.alloc("O1S", [64, TOK], BF16)
    ONE1 = sb.alloc("ONE1", [1, 128], BF16)
    ONEC = sb.alloc("ONEC", [128, 1], BF16)
    KN2 = sb.alloc("KN2", [1, NKV], F32)
    QN2 = sb.alloc("QN2", [1, 32], F32)
    TR1 = sb.alloc("TR1", [1, 1], F32)
    TR2 = sb.alloc("TR2", [1, 1], F32)
    BNDB = sb.alloc("BNDB", [1, 32], BF16)
    BIASC = sb.alloc("BIASC", [128, 32], F32)
    RZ = sb.alloc("RZ", [128, 1], F32)
    M0 = sb.alloc("M0", [128, FCH // QBLK], F32)
    M1 = sb.alloc("M1", [128, FCH // QBLK], F32)
    M2 = sb.alloc("M2", [128, FCH // QBLK], F32)
    QS["M0"], QS["M1"], QS["M2"] = M0, M1, M2

    ps = [nc.alloc_psum_tensor(f"ps{i}", [128, 512], F32) for i in range(8)]
    psT = ps[6][:, 0:64].bitcast(BF16)

    LD, STs, MM, GP = "ld", "st", "mm", "gp"

    p.dma("sync", KTS[:].rearrange("p (h k) -> p h k", k=S),
          ktb.rearrange("(h p) k -> p h k", p=128), LD)
    p.dma("sync", VAS[:].rearrange("p (t c) -> p t c", c=1032),
          vaug.rearrange("(t p) c -> p t c", p=128), LD)
    p.dma("sync", QTS[:].rearrange("p (h q) -> p h q", q=TOK),
          qt.rearrange("(h p) q -> p h q", p=128), LD)
    p.dma("sync", MSK[:].rearrange("p (s j q) -> p s j q", j=2, q=128),
          mskr.rearrange("s j p q -> p s j q"), LD)
    p.dma("sync", UC[:], ucst, LD)
    p.dma("sync", IDN[:], idn, LD)
    p.dma("sync", O1S[0:64, :], o1t, LD)
    p.dma("sync", ONE1[0:1, :], one1, LD)
    ldv = p.dma("sync", ONEC[:], onec, LD)
    p.wait("scalar", LD, ldv)
    p.wait("tensor", LD, ldv)
    p.wait("vector", LD, ldv)

    def norms_for(block_ap, width, out_row, out_col):
        p.wait("scalar", MM, p.val(MM))
        scv = p.op("scalar", lambda e, _b=block_ap, _w=width: e.activation(
            KSQ[:, 0:_w], _b, AF.Square))
        p.wait("tensor", "sc", scv)
        for cidx in range(width // 512):
            mmv = p.op("tensor", lambda e, _c=cidx: e.matmul(
                ps[7][0:1, 0:512], ONEC[:],
                KSQ[:, 512 * _c:512 * (_c + 1)], start=True, stop=True), MM)
            p.wait("vector", MM, mmv)
            if cidx == 0:
                vcv = p.op("vector", lambda e: e.tensor_reduce(
                    TR1[0:1, :], ps[7][0:1, 0:512], AX.X, ALU.max))
            else:
                p.op("vector", lambda e: e.tensor_reduce(
                    TR2[0:1, :], ps[7][0:1, 0:512], AX.X, ALU.max))
                vcv = p.op("vector", lambda e: e.tensor_tensor(
                    TR1[0:1, :], TR1[0:1, :], TR2[0:1, :], ALU.max))
            p.wait("tensor", "vc", vcv)
        p.op("vector", lambda e, _o=out_col, _r=out_row: e.tensor_copy(
            _r[0:1, _o:_o + 1], TR1[0:1, :]))

    for h in range(NKV):
        norms_for(KTS[:, S * h:S * (h + 1)], S, KN2, h)
    for H in range(32):
        norms_for(QTS[:, TOK * H:TOK * (H + 1)], TOK, QN2, H)

    for kvh in range(NKV):
        p.op("vector", lambda e, _h=kvh: e.tensor_reduce(
            TR1[0:1, :], QN2[0:1, 4 * _h:4 * (_h + 1)], AX.X, ALU.max))
        p.op("vector", lambda e, _h=kvh: e.tensor_tensor(
            TR1[0:1, :], TR1[0:1, :], KN2[0:1, _h:_h + 1], ALU.add))
        vcv = p.op("vector", lambda e, _h=kvh: e.tensor_scalar(
            BNDB[0:1, _h:_h + 1], TR1[0:1, :], -SCALE / 2.0, None, ALU.mult))
        p.wait("tensor", "vc", vcv)
        mmv = p.op("tensor", lambda e, _h=kvh: e.matmul(
            ps[7][:, 0:1], ONE1[0:1, :], BNDB[0:1, _h:_h + 1],
            start=True, stop=True), MM)
        p.wait("vector", MM, mmv)
        vcv = p.op("vector", lambda e, _h=kvh: e.tensor_copy(
            BIASC[:, _h:_h + 1], ps[7][:, 0:1]))
        p.wait("tensor", "vc", vcv)

    for kvh in range(NKV):
        for slot in range(8):
            E = EXT[slot]
            exp_done = [None] * E

            def sc_stage2(kt):
                pa = ps[kt % 2]
                if kt >= 2:
                    p.wait("tensor", "sc", exp_done[kt - 2])
                lm = kt >= E - 2
                last_mm = None
                for qh in range(4):
                    H = 4 * kvh + qh
                    is_last = (qh == 3) and not lm
                    last_mm = p.op("tensor",
                                   lambda e, _kvh=kvh, _H=H, _s=slot, _kt=kt,
                                   _pa=pa, _lm=lm, _qh=qh: e.matmul(
                                       _pa[:, 128 * _qh:128 * (_qh + 1)],
                                       KTS[:, S * _kvh + 128 * _kt:
                                           S * _kvh + 128 * (_kt + 1)],
                                       QTS[:, TOK * _H + 128 * _s:
                                           TOK * _H + 128 * (_s + 1)],
                                       start=(_qh == 0),
                                       stop=(_qh == 3 and not _lm)),
                                   MM if is_last else None)
                    if lm:
                        j = kt - (E - 2)
                        last_mm = p.op("tensor",
                                       lambda e, _pa=pa, _s=slot, _j=j,
                                       _qh=qh: e.matmul(
                                           _pa[:, 128 * _qh:128 * (_qh + 1)],
                                           UC[:],
                                           MSK[:, (2 * _s + _j) * 128:
                                               (2 * _s + _j + 1) * 128],
                                           start=False, stop=(_qh == 3)),
                                       MM if qh == 3 else None)
                return last_mm

            def exp_stage(kt, mmv):
                pa = ps[kt % 2]
                p.wait("scalar", MM, mmv)
                ex = p.op("scalar", lambda e, _pa=pa, _k=kt, _h=kvh: e.activation(
                    PT[_k % 2][:, 0:512], _pa[:, 0:512], AF.Exp,
                    scale=SCALE, bias=BIASC[:, _h:_h + 1]))
                exp_done[kt] = ex

            def av_stage(kt):
                pat0 = 2
                p.wait("tensor", "sc", exp_done[kt])
                lastv = None
                for qh in range(4):
                    lastv = p.op("tensor",
                                 lambda e, _k=kt, _kvh=kvh, _qh=qh, _E=E: e.matmul(
                                     ps[2 + _qh][:, 0:129], PT[_k % 2][:, 128 * _qh:128 * (_qh + 1)],
                                     VAS[:, 1032 * _k + 129 * _kvh:
                                         1032 * _k + 129 * (_kvh + 1)],
                                     start=(_k == 0), stop=(_k == _E - 1)),
                                 MM if (kt == E - 1 and qh == 3) else None)
                return lastv

            mm_prev = sc_stage2(0)
            exp_stage(0, mm_prev)
            for kt in range(1, E):
                mm_prev = sc_stage2(kt)
                av_stage(kt - 1)
                exp_stage(kt, mm_prev)
            av_last = av_stage(E - 1)

            p.wait("vector", MM, av_last)
            for qh in range(4):
                H = 4 * kvh + qh
                p.op("vector", lambda e, _q=qh: e.reciprocal(
                    RZ[:], ps[2 + _q][:, 128:129]))
                vcv = p.op("vector", lambda e, _q=qh: e.tensor_scalar(
                    ATL[:], ps[2 + _q][:, 0:128], RZ[:], None, ALU.mult))
                p.wait("tensor", "vc", vcv)
                mmv = p.op("tensor", lambda e: e.transpose(
                    psT[:, 0:128], ATL[:], IDN[:]), MM)
                p.wait("vector", MM, mmv)
                vcv = p.op("vector", lambda e: e.tensor_copy(
                    ATLT[:], psT[:, 0:128]))
                p.wait("tensor", "vc", vcv)
                p.wait("sync", "vc", vcv)
                stv = p.dma("sync",
                            attnt_d[:, TOK * H + 128 * slot:
                                    TOK * H + 128 * (slot + 1)],
                            ATLT[:], STs)
                p.wait("vector", STs, stv)

    p.phase_barrier()

    xq_ld = p.dma("sync", XQS[:], xqt, "xq")
    g_ld, u_ld = {}, {}
    g_mm, u_mm = {}, {}
    silu_done, hev, qdq_done = {}, {}, {}
    tdone, tcopy_done, st_done = {}, {}, {}
    h_sc_guard = {}

    def load_g(ch):
        buf = ch % 2
        if ch >= 2:
            p.wait("sync", MM, g_mm[(ch - 2, 7)])
        g_ld[ch] = p.dma("sync", WGC[buf][:],
                         wgt[128 * ch:128 * (ch + 1), :],
                         "wga" if buf == 0 else "wgb")

    def load_u(ch):
        if ch >= 1:
            p.wait("sync", MM, u_mm[(ch - 1, 7)])
        u_ld[ch] = p.dma("sync", WUC[:], wut[128 * ch:128 * (ch + 1), :],
                         "wua" if ch % 2 == 0 else "wub")

    def trans_block(ch, tt):
        w = CW[ch]
        hq = HQB[tt % 4]
        pv = ps[tt][:, 0:64].bitcast(BF16)
        p.wait("tensor", "vc", qdq_done[(ch, tt)])
        if tt == 0 and ch >= 1:
            p.wait("scalar", *st_done[ch - 1])
        for j in range(w // 128):
            mmv = p.op("tensor", lambda e, _j=j, _h=hq, _pv=pv: e.transpose(
                _pv[:, 0:128], _h[:, 128 * _j:128 * (_j + 1)], IDN[:]), MM)
            p.wait("scalar", MM, mmv)
            scv = p.op("scalar", lambda e, _j=j, _tt=tt, _pv=pv:
                       e.activation(
                           HQTW[_j][:, 128 * _tt:128 * (_tt + 1)],
                           _pv[:, 0:128], AF.Copy))
            p.wait("tensor", "sc", scv)
        tdone[(ch, tt)] = mmv
        tcopy_done[(ch, tt)] = scv
        if tt == 7:
            p.wait("sync", "sc", scv)
            ssem = "sta" if ch % 2 == 0 else "stb"
            for j in range(w // 128):
                p.dma("sync",
                      hqt_d[FCH * ch + 128 * j:FCH * ch + 128 * (j + 1), :],
                      HQTW[j][:], ssem)
            st_done[ch] = (ssem, p.val(ssem))

    def g_block(ch, tt):
        buf = ch % 2
        w = CW[ch]
        if ch > 0 and tt >= 5:
            trans_block(ch - 1, tt)
        if tt == 0:
            p.wait("tensor", "wga" if buf == 0 else "wgb", g_ld[ch])
            if ch == 0:
                p.wait("tensor", "xq", xq_ld)
        if ch > 0:
            p.wait("tensor", "sc", tcopy_done[(ch - 1, tt)])
        for dt in range(32):
            mmv = p.op("tensor", lambda e, _dt=dt, _tt=tt, _b=buf, _w=w:
                       e.matmul(
                           ps[_tt][:, 0:_w],
                           XQS[:, TOK * _dt + 128 * _tt:
                               TOK * _dt + 128 * (_tt + 1)],
                           WGC[_b][:, FCH * _dt:FCH * _dt + _w],
                           start=(_dt == 0), stop=(_dt == 31)),
                       MM if dt == 31 else None)
        g_mm[(ch, tt)] = mmv
        p.wait("scalar", MM, mmv)
        if ch > 0:
            p.wait("scalar", "vc", hev[(ch - 1, tt)])
        silu_done[(ch, tt)] = p.op(
            "scalar", lambda e, _tt=tt, _w=w: e.activation(
                SILB[:, FCH * _tt:FCH * _tt + _w], ps[_tt][:, 0:_w],
                AF.Silu))

    def u_block(ch, tt):
        w = CW[ch]
        if tt >= 3:
            trans_block(ch, tt - 3)
        if tt == 0:
            p.wait("tensor", "wua" if ch % 2 == 0 else "wub", u_ld[ch])
        p.wait("tensor", "sc", silu_done[(ch, tt)])
        for dt in range(32):
            mmv = p.op("tensor", lambda e, _dt=dt, _tt=tt, _w=w: e.matmul(
                ps[_tt][:, 0:_w],
                XQS[:, TOK * _dt + 128 * _tt:TOK * _dt + 128 * (_tt + 1)],
                WUC[:, FCH * _dt:FCH * _dt + _w],
                start=(_dt == 0), stop=(_dt == 31)),
                MM if dt == 31 else None)
        u_mm[(ch, tt)] = mmv
        p.wait("vector", MM, mmv)
        prevk = (ch, tt - 4) if tt >= 4 else (ch - 1, tt + 4)
        if prevk in tdone:
            p.wait("vector", MM, tdone[prevk])
        p.wait("vector", "sc", p.val("sc"))
        vcv = p.op("vector", lambda e, _tt=tt, _w=w: e.tensor_tensor(
            H_[:, 0:_w], SILB[:, FCH * _tt:FCH * _tt + _w],
            ps[_tt][:, 0:_w], ALU.mult))
        hev[(ch, tt)] = vcv
        p.wait("scalar", "vc", vcv)
        emit_qdq(p, QS, H_[:, 0:w], HQB[tt % 4][:, 0:w], 128, w, "sc")
        if ch == NFCH - 1:
            p.op("vector", lambda e, _tt=tt, _w=w: e.tensor_copy(
                HQB[_tt % 4][:, _w - SEL:_w], H_[:, _w - SEL:_w]))
        qdq_done[(ch, tt)] = p.op("vector", lambda e: e.engine_nop())

    load_g(0)
    load_u(0)
    load_g(1)
    for ch in range(NFCH):
        for tt in range(8):
            g_block(ch, tt)
        if ch + 2 < NFCH:
            load_g(ch + 2)
        for tt in range(8):
            u_block(ch, tt)
        if ch + 1 < NFCH:
            load_u(ch + 1)
    for tt in range(5, 8):
        trans_block(NFCH - 1, tt)

    p.phase_barrier()

    att_ld = p.dma("gpsimd", ATT2F[:], attnt_d, "gat")

    evac_c1 = {}
    mm_c1 = {}
    wd_ld = {}
    wd_sem = ["wd", "we"]

    def c1_load_wdc(sl):
        par = sl % 2
        if sl >= 2:
            p.wait("sync", MM, mm_c1.get(sl - 2, 0))
        wd_ld[sl] = p.dma("sync", WDC[par][:],
                          wdt[128 * sl:128 * (sl + 1), :], wd_sem[par])

    c1_load_wdc(0)
    c1_load_wdc(1)
    hq_sems = ["hqa", "hqb", "hqc", "hqd"]
    hq_ld = {}
    hq_rd = {}

    def c1_load_hq(gi):
        b = gi % 4
        sl_, ft_ = divmod(gi, FHALF)
        fh_ = sl_ % 2
        prev = hq_rd.get(gi - 4)
        if prev:
            p.wait("sync", MM, prev)
        r0 = FHALF * 128 * fh_ + 128 * ft_
        hq_ld[gi] = p.dma("sync", HQF[b][:], hqt_d[r0:r0 + 128, :],
                          hq_sems[b])

    for pre in range(4):
        c1_load_hq(pre)
    ds_sems = ["dsa", "dsb"]
    ds_val = {}
    for dc in range(8):
        for fh in range(2):
            sl = dc * 2 + fh
            par = sl % 2
            for ft in range(FHALF):
                gi = sl * FHALF + ft
                b = gi % 4
                p.wait("tensor", hq_sems[b], hq_ld[gi])
                if ft == 0:
                    p.wait("tensor", wd_sem[par], wd_ld[sl])
                for tt in range(8):
                    if fh == 0 and ft == 0 and dc > 0:
                        p.wait("tensor", "vc", evac_c1[(dc - 1, tt)])
                    mmv = p.op("tensor",
                               lambda e, _ft=ft, _fh=fh, _b=b, _tt=tt,
                               _par=par: e.matmul(
                                   ps[_tt][:, 0:512],
                                   HQF[_b][:, 128 * _tt:128 * (_tt + 1)],
                                   WDC[_par][:, 512 * _ft:512 * (_ft + 1)],
                                   start=(_ft == 0 and _fh == 0),
                                   stop=(_ft == FHALF - 1 and _fh == 1)),
                               MM if tt == 7 else None)
                hq_rd[gi] = mmv
                mm_c1[sl] = mmv
                if gi + 4 < 16 * FHALF:
                    c1_load_hq(gi + 4)
            if sl + 2 < 16:
                c1_load_wdc(sl + 2)
        p.wait("vector", MM, mm_c1[dc * 2 + 1])
        for tt in range(8):
            dpar = tt % 2
            pv = ds_val.get((dc, tt - 2), ds_val.get((dc - 1, tt + 6)))
            if pv is not None:
                p.wait("vector", ds_sems[dpar], pv)
            vcv = p.op("vector", lambda e, _tt=tt, _dp=dpar: e.tensor_copy(
                DSTG[_dp][:], ps[_tt][:, 0:512]))
            evac_c1[(dc, tt)] = vcv
            p.wait("sync", "vc", vcv)
            ds_val[(dc, tt)] = p.dma(
                "sync", down_d[128 * tt:128 * (tt + 1),
                               512 * dc:512 * (dc + 1)],
                DSTG[dpar][:], ds_sems[dpar])

    p.phase_barrier()

    lr_ld = p.dma("gpsimd", LRB[0:64, :], lrb, "glr")
    ost_val = {}
    dt_sem = ["dtm", "dtn"]
    xt_sem = ["xt", "xu"]
    c2_ld = {}
    c2_mm = {}
    c2_ev = {}

    def c2_load(gi):
        dc, tt = divmod(gi, 8)
        par = gi % 2
        prev_ev = c2_ev.get(gi - 2)
        if prev_ev:
            p.wait("sync", "vc", prev_ev)
        d = p.dma("sync", DT2[par][:], down_d[128 * tt:128 * (tt + 1),
                                              512 * dc:512 * (dc + 1)],
                  dt_sem[par])
        x = p.dma("sync", XT2[par][:], xres[128 * tt:128 * (tt + 1),
                                            512 * dc:512 * (dc + 1)],
                  xt_sem[par])
        c2_ld[gi] = (d, x)

    gw_sem = ["gwa", "gwb"]
    for dc in range(8):
        b = dc % 2
        if dc >= 2:
            p.wait("gpsimd", MM, c2_mm[(dc - 2) * 8 + 7])
        gv = p.dma("gpsimd", WOC[b][:],
                   wo[128 * dc:128 * (dc + 1), :], gw_sem[b])
        if dc == 0:
            c2_load(0)
            c2_load(1)
        p.wait("tensor", gw_sem[b], gv)
        if dc == 0:
            p.wait("tensor", "glr", lr_ld)
            p.wait("tensor", "gat", att_ld)
        for tt in range(8):
            gi = dc * 8 + tt
            par = gi % 2
            d, x = c2_ld[gi]
            ev_prev = c2_ev.get(gi - 2)
            if ev_prev:
                p.wait("tensor", "vc", ev_prev)
            for dt in range(32):
                p.op("tensor", lambda e, _dt=dt, _b=b, _tt=tt, _par=par:
                     e.matmul(
                         ps[_par][:, 0:512],
                         ATT2F[:, TOK * _dt + 128 * _tt:
                               TOK * _dt + 128 * (_tt + 1)],
                         WOC[_b][:, 512 * _dt:512 * (_dt + 1)],
                         start=(_dt == 0), stop=False))
            mmv = p.op("tensor", lambda e, _b=b, _tt=tt, _dc=dc, _par=par: e.matmul(
                ps[_par][:, 0:512], O1S[0:64, 128 * _tt:128 * (_tt + 1)],
                LRB[0:64, 512 * _dc:512 * (_dc + 1)],
                start=False, stop=True), MM)
            c2_mm[gi] = mmv
            p.wait("vector", MM, mmv)
            p.wait("vector", dt_sem[par], d)
            p.wait("vector", xt_sem[par], x)
            stp = c2_ev.get(gi - 2)
            if stp is not None:
                p.wait("vector", "ost" if (gi - 2) % 2 == 0 else "osu",
                       ost_val[gi - 2])
            p.op("vector", lambda e, _par=par: e.tensor_tensor(
                OS2[_par][:], ps[_par][:, 0:512], DT2[_par][:], ALU.add))
            vcv = p.op("vector", lambda e, _par=par: e.tensor_tensor(
                OS2[_par][:], OS2[_par][:], XT2[_par][:], ALU.add))
            c2_ev[gi] = vcv
            p.wait("sync", "vc", vcv)
            ost_val[gi] = p.dma("sync", out[128 * tt:128 * (tt + 1),
                                            512 * dc:512 * (dc + 1)],
                                OS2[par][:], "ost" if gi % 2 == 0 else "osu")
            if gi + 2 < 64:
                c2_load(gi + 2)

    p.wait("sync", "ost", p.val("ost"))
    p.wait("sync", "osu", p.val("osu"))

    sems = {}
    stack = contextlib.ExitStack()
    for name in (LD, STs, "vc", "sc", MM, GP, "gpc", "xq", "wga", "wgb",
                 "wua", "wub",
                 "sta", "stb", "wd", "we", "hqa", "hqb", "hqc", "hqd",
                 "dsa", "dsb", "gat", "dtm",
                 "dtn", "xt", "xu", "ost", "osu", "glr", "gwa", "gwb"):
        sems[name] = stack.enter_context(nc.semaphore(name))
    with nc.Block() as block:
        p.replay(block, sems)
    stack.close()
    return nc


def _launch2_inputs(inputs, res1, PR):
    X = np.asarray(inputs["L0_attn_hidden_states"]).reshape(NB * S, D)
    up = np.asarray(inputs["L0_ffn_up_reorder_index"]).astype(np.int64)
    dn = np.asarray(inputs["L0_ffn_down_reorder_index"]).astype(np.int64)

    wg_full = np.concatenate([res1[c]["wg_o"] for c in range(NCORES)], axis=0)
    wu_full = np.concatenate([res1[c]["wu_o"] for c in range(NCORES)], axis=0)
    wd_full = np.concatenate([res1[c]["wd_o"] for c in range(NCORES)], axis=1)
    WGT = wg_full.T
    WUT = wu_full.T
    WDT = wd_full.T
    def _blk_gu(W):
        W2 = np.zeros((NFCH, 128, 32, FCH), dtype=BF)
        for ch in range(NFCH):
            w = CW[ch]
            blk = W[:, FCH * ch:FCH * ch + w].reshape(32, 128, w)
            W2[ch, :, :, :w] = blk.transpose(1, 0, 2)
        return np.ascontiguousarray(W2.reshape(NFCH * 128, 32 * FCH))

    WGT2 = _blk_gu(WGT)
    WUT2 = _blk_gu(WUT)
    WDT2 = np.ascontiguousarray(
        WDT.reshape(2, FHALF, 128, 8, 512).transpose(3, 0, 2, 1, 4)
        .reshape(16 * 128, FHALF * 512))

    ucst = (np.arange(128)[None, :] >= np.arange(128)[:, None]).astype(BF)
    one1 = np.ones((1, 128), dtype=BF)
    onec = np.ones((128, 1), dtype=BF)
    idn = np.eye(128, dtype=BF)
    wo = np.asarray(inputs["L0_attn_w_o"], np.float32)
    WO2 = np.ascontiguousarray(
        wo.reshape(32, 128, 8, 512).transpose(2, 1, 0, 3)
        .reshape(8 * 128, 32 * 512))
    lrbi = np.ascontiguousarray(np.asarray(inputs["L0_ffn_lr_B"], np.float32))

    in_maps = []
    for c in range(NCORES):
        b, par = c // 2, c % 2
        slots = SLOT_A if par == 0 else SLOT_B
        pair = [2 * b, 2 * b + 1]
        pr_pair = PR[TOK * pair[0]:TOK * (pair[0] + 1)].tolist() +                   PR[TOK * pair[1]:TOK * (pair[1] + 1)].tolist()
        rel = np.asarray(pr_pair, dtype=np.int64) - 2048 * b
        inv = np.empty(S, dtype=np.int64)
        inv[rel] = np.arange(S)
        kt_cat = np.concatenate([res1[pair[0]]["kt_o"],
                                 res1[pair[1]]["kt_o"]], axis=1)
        ktb = np.ascontiguousarray(kt_cat[:, inv])
        v_cat = np.concatenate([res1[pair[0]]["v_o"],
                                res1[pair[1]]["v_o"]], axis=0)
        v_nat = v_cat[inv, :]
        vaug = np.empty((S, NKV * 129), dtype=BF)
        for h in range(NKV):
            vaug[:, 129 * h:129 * h + 128] = v_nat[:, 128 * h:128 * (h + 1)]
            vaug[:, 129 * h + 128] = np.ones(S, dtype=BF)
        mskr = np.zeros((8, 2, 128, 128), dtype=np.float32)
        for si, t0 in enumerate(slots):
            E = EXT[si]
            for j in range(2):
                kt = E - 2 + j
                off = 128 * (t0 - kt)
                for q in range(128):
                    thr = q + off + 1
                    if thr <= 0:
                        mskr[si, j, 0, q] = -BIGM
                    elif thr <= 127:
                        mskr[si, j, thr, q] = -BIGM
        xqt2 = np.ascontiguousarray(
            res1[c]["xq_o"].T.reshape(32, 128, TOK).transpose(1, 0, 2)
            .reshape(128, 32 * TOK))
        in_maps.append({
            "xres": np.ascontiguousarray(X[PR[TOK * c:TOK * (c + 1)]]),
            "xqt": xqt2,
            "qt": res1[c]["qt_o"],
            "ktb": ktb,
            "vaug": vaug,
            "o1t": res1[c]["o1t_o"],
            "lrb": lrbi,
            "wo": WO2,
            "wgt": WGT2, "wut": WUT2, "wdt": WDT2,
            "mskr": mskr.astype(BF),
            "ucst": ucst, "one1": one1, "onec": onec, "idn": idn,
        })
    return in_maps


def kernel(**inputs):
    in_maps, PR = _launch1_inputs(inputs)
    nc1 = build_launch1()
    res1o = run_bass_kernel_spmd(nc1, in_maps, core_ids=list(range(NCORES)))
    res1 = res1o.results
    in_maps2 = _launch2_inputs(inputs, res1, PR)
    nc2 = build_launch2()
    res2 = run_bass_kernel_spmd(nc2, in_maps2, core_ids=list(range(NCORES)))
    out = np.empty((NB * S, D), dtype=np.float32)
    for c in range(NCORES):
        out[PR[TOK * c:TOK * (c + 1)]] = res2.results[c]["out"]
    return out.reshape(NB, S, D)


# version 09
# speedup vs baseline: 1.2723x; 1.2723x over previous
import contextlib
import math
import numpy as np
import ml_dtypes
import concourse.bass as bass
import concourse.mybir as mybir
from concourse.bass_utils import run_bass_kernel_spmd

F32 = mybir.dt.float32
I32 = mybir.dt.int32
BF16 = mybir.dt.bfloat16
U8 = mybir.dt.uint8
ALU = mybir.AluOpType
AF = mybir.ActivationFunctionType
AX = mybir.AxisListType
BF = ml_dtypes.bfloat16

NCORES = 8
D = 4096
DQ = 4096
DKV = 1024
HD = 128
NKV = 8
GQ = 4
FF = 11008
FSH = FF // NCORES
TOK = 1024
NB = 4
S = 2048
R = 64
SEL = 128
QBLK = 16
BIGM = 30000.0
SCALE = float(128.0 ** -0.5)

SLOT_A = [0, 3, 4, 7, 8, 11, 12, 15]
SLOT_B = [1, 2, 5, 6, 9, 10, 13, 14]
EXT = [2, 4, 6, 8, 10, 12, 14, 16]


def _row_perm():
    rows = []
    for b in range(NB):
        for slots in (SLOT_A, SLOT_B):
            for t in slots:
                base = 2048 * b + 128 * t
                rows.extend(range(base, base + 128))
    return np.asarray(rows, dtype=np.int64)



ENGINES = ("sync", "gpsimd", "vector", "scalar", "tensor")


class Prog:

    def __init__(self):
        self.ops = {e: [] for e in ENGINES}
        self.cnt = {}

    def bump(self, sem, by):
        self.cnt[sem] = self.cnt.get(sem, 0) + by
        return self.cnt[sem]

    def val(self, sem):
        return self.cnt.get(sem, 0)

    def dma(self, eng, out, in_, sem, **kw):
        tot = self.bump(sem, 16)

        def fn(e, _o=out, _i=in_, _s=sem, _k=kw):
            e.dma_start(out=_o, in_=_i, **_k).then_inc(self.sems[_s], 16)
        self.ops[eng].append(fn)
        return tot

    def wait(self, eng, sem, val):
        def fn(e, _s=sem, _v=val):
            e.wait_ge(self.sems[_s], _v)
        self.ops[eng].append(fn)

    CHAIN = {"vector": "vc", "scalar": "sc", "gpsimd": "gpc"}

    def op(self, eng, fn, sem=None):
        ch = self.CHAIN.get(eng)
        if ch is not None:
            self.wait(eng, ch, self.val(ch))
            tot = self.bump(ch, 1)

            def g(e, _f=fn, _c=ch):
                _f(e).then_inc(self.sems[_c], 1)
            self.ops[eng].append(g)
            return tot
        if sem is None:
            self.ops[eng].append(lambda e, _f=fn: _f(e))
            return None
        tot = self.bump(sem, 1)

        def g(e, _f=fn, _s=sem):
            _f(e).then_inc(self.sems[_s], 1)
        self.ops[eng].append(g)
        return tot

    def phase_barrier(self, engines=ENGINES):
        targets = [(n, v) for n, v in sorted(self.cnt.items()) if v]
        for eng in engines:
            for n, v in targets:
                self.wait(eng, n, v)

    def barrier(self, producer_eng, sem, waiters):
        tot = self.op(producer_eng, lambda e: e.engine_nop()
                      if producer_eng in ("vector", "scalar") else e.nop(), sem)
        for w in waiters:
            self.wait(w, sem, tot)
        return tot

    def replay(self, block, sems):
        self.sems = sems

        def mk(eng):
            def run(e):
                for fn in self.ops[eng]:
                    fn(e)
            return run
        block.sync(mk("sync"))
        block.gpsimd(mk("gpsimd"))
        block.vector(mk("vector"))
        block.scalar(mk("scalar"))
        block.tensor(mk("tensor"))


class Sbuf:
    BASE = 16544
    LIMIT = 229376

    def __init__(self, nc):
        self.nc = nc
        self.off = self.BASE

    def alloc(self, name, shape, dtype):
        nbytes = int(np.prod(shape[1:])) * mybir.dt.size(dtype)
        off = (self.off + 31) // 32 * 32
        assert off + nbytes <= self.LIMIT, (
            f"SBUF overflow: {name} at {off}+{nbytes} > {self.LIMIT}")
        t = self.nc.alloc_sbuf_tensor_at(name, list(shape), dtype, offset=off)
        self.off = off + nbytes
        return t


def _nop(eng_name):
    def fn(e):
        return e.engine_nop() if hasattr(e, "engine_nop") else e.nop()
    return fn



def emit_qdq(p, scr, v_ap, out_ap, P, F, SC="sc", scale_ap=None,
             use_gpsimd=False):
    A, Bs, Cs, Es, Ii, M0, M1, M2, SG = (
        scr["A"], scr["B"], scr["C"], scr["E"], scr["I"],
        scr["M0"], scr["M1"], scr["M2"], scr["SG"])
    nb = F // QBLK
    v3 = v_ap.rearrange("p (b c) -> p b c", c=QBLK)

    p.op("scalar", lambda e: e.activation(SG[:P, :F], v_ap, AF.Sign))
    sgv = p.op("scalar", lambda e: e.activation(Es[:P, :F], v_ap, AF.Abs), SC)

    p.op("vector", lambda e: e.tensor_reduce(
        M0[:P, :nb].unsqueeze(2), v3, AX.X, ALU.max, apply_absolute_value=True))
    p.op("vector", lambda e: e.tensor_scalar(
        M0[:P, :nb], M0[:P, :nb], 1e-30, None, ALU.max))
    p.op("vector", lambda e: e.reciprocal(M1[:P, :nb], M0[:P, :nb]))
    p.op("vector", lambda e: e.tensor_scalar(
        M1[:P, :nb], M1[:P, :nb], 6.0, None, ALU.mult))
    p.op("vector", lambda e: e.tensor_scalar(
        M2[:P, :nb], M0[:P, :nb], 1.0 / 6.0, None, ALU.mult))
    m1b = M1[:P, :nb].unsqueeze(2).broadcast_to([P, nb, QBLK])
    m2b = M2[:P, :nb].unsqueeze(2).broadcast_to([P, nb, QBLK])

    p.wait("vector", SC, sgv)
    tv = p.op("vector", lambda e: e.tensor_tensor(
        Bs[:P, :F].rearrange("p (b c) -> p b c", c=QBLK),
        Es[:P, :F].rearrange("p (b c) -> p b c", c=QBLK), m1b, ALU.mult))

    geng = "gpsimd" if use_gpsimd else "vector"
    gch = "gpc" if use_gpsimd else "vc"
    If = Ii[:P, :F].bitcast(F32)
    p.op("vector", lambda e: e.tensor_scalar(
        Cs[:P, :F], Bs[:P, :F], 0.25, None, ALU.is_ge))
    for b in (0.75, 1.25, 1.75):
        p.op("vector", lambda e, _b=b: e.scalar_tensor_tensor(
            Cs[:P, :F], Bs[:P, :F], _b, Cs[:P, :F], ALU.is_ge, ALU.add))
    if use_gpsimd:
        p.wait(geng, "vc", tv)
    p.op(geng, lambda e: e.tensor_scalar(
        Es[:P, :F], Bs[:P, :F], 2.5, None, ALU.is_ge))
    p.op(geng, lambda e: e.scalar_tensor_tensor(
        Es[:P, :F], Bs[:P, :F], 3.5, Es[:P, :F], ALU.is_ge, ALU.add))
    gv = p.op(geng, lambda e: e.tensor_scalar(
        If, Bs[:P, :F], 5.0, 2.0, ALU.is_ge, ALU.mult))
    if use_gpsimd:
        p.wait("vector", gch, gv)
    p.op("vector", lambda e: e.scalar_tensor_tensor(
        Cs[:P, :F], Cs[:P, :F], 0.5, Es[:P, :F], ALU.mult, ALU.add))
    p.op("vector", lambda e: e.tensor_tensor(
        Cs[:P, :F], Cs[:P, :F], If, ALU.add))
    p.op("vector", lambda e: e.tensor_tensor(
        Bs[:P, :F].rearrange("p (b c) -> p b c", c=QBLK),
        SG[:P, :F].rearrange("p (b c) -> p b c", c=QBLK), m2b, ALU.mult))
    if scale_ap is not None:
        p.op("vector", lambda e: e.scalar_tensor_tensor(
            Cs[:P, :F], Cs[:P, :F], scale_ap, Bs[:P, :F], ALU.mult, ALU.mult))
        vcv = p.op("vector", lambda e: e.tensor_copy(out_ap, Cs[:P, :F]))
    else:
        vcv = p.op("vector", lambda e: e.tensor_tensor(
            out_ap, Cs[:P, :F], Bs[:P, :F], ALU.mult))
    return vcv



def build_launch1():
    nc = bass.Bass("TRN2", target_bir_lowering=False)
    sb = Sbuf(nc)
    p = Prog()

    xp = nc.dram_tensor("xp", [TOK, D], F32, kind="ExternalInput").ap()
    wq = nc.dram_tensor("wq", [D, DQ], F32, kind="ExternalInput").ap()
    wk = nc.dram_tensor("wk", [D, DKV], F32, kind="ExternalInput").ap()
    wv = nc.dram_tensor("wv", [D, DKV], F32, kind="ExternalInput").ap()
    la = nc.dram_tensor("la", [D, R], F32, kind="ExternalInput").ap()
    wgs = nc.dram_tensor("wgs", [FSH, D], F32, kind="ExternalInput").ap()
    wus = nc.dram_tensor("wus", [FSH, D], F32, kind="ExternalInput").ap()
    wds = nc.dram_tensor("wds", [D, FSH], F32, kind="ExternalInput").ap()
    wdm = nc.dram_tensor("wdm", [128, FSH], U8, kind="ExternalInput").ap()
    osc = nc.dram_tensor("osc", [128, D // 128], F32, kind="ExternalInput").ap()
    idn = nc.dram_tensor("idn", [128, 128], BF16, kind="ExternalInput").ap()

    xq_o = nc.dram_tensor("xq_o", [TOK, D], BF16, kind="ExternalOutput").ap()
    qt_o = nc.dram_tensor("qt_o", [DQ, TOK], BF16, kind="ExternalOutput").ap()
    kt_o = nc.dram_tensor("kt_o", [DKV, TOK], BF16, kind="ExternalOutput").ap()
    v_o = nc.dram_tensor("v_o", [TOK, DKV], BF16, kind="ExternalOutput").ap()
    o1t_o = nc.dram_tensor("o1t_o", [R, TOK], BF16, kind="ExternalOutput").ap()
    wg_o = nc.dram_tensor("wg_o", [FSH, D], BF16, kind="ExternalOutput").ap()
    wu_o = nc.dram_tensor("wu_o", [FSH, D], BF16, kind="ExternalOutput").ap()
    wd_o = nc.dram_tensor("wd_o", [D, FSH], BF16, kind="ExternalOutput").ap()

    xn_d = nc.dram_tensor("xn_d", [TOK, D], BF16).ap()

    A = sb.alloc("A", [128, D], F32)
    Bt = sb.alloc("B", [128, D], F32)
    Ct = sb.alloc("C", [128, D], F32)
    Et = sb.alloc("E", [128, D], F32)
    It = sb.alloc("I", [128, D], I32)
    SG = sb.alloc("SG", [128, D], F32)
    XA = [sb.alloc(f"XA{i}", [128, D], F32) for i in range(2)]
    XSQ = sb.alloc("XSQ", [128, D], F32)
    assert sb.off <= Sbuf.BASE + 160 * 1024
    XNT = nc.alloc_sbuf_tensor_at("XNT", [128, 32 * TOK], BF16, offset=Sbuf.BASE)
    WB = [nc.alloc_sbuf_tensor_at(f"WB{i}", [128, 32 * 128], BF16,
                                  offset=Sbuf.BASE + 65536 + i * 8192) for i in range(2)]
    QSTG = [nc.alloc_sbuf_tensor_at(f"QSTG{i}", [128, TOK], BF16,
                                    offset=Sbuf.BASE + 81920 + 2048 * i)
            for i in range(2)]
    VSTG = [nc.alloc_sbuf_tensor_at(f"VSTG{i}", [128, TOK], BF16,
                                    offset=Sbuf.BASE + 86016 + 2048 * i)
            for i in range(2)]
    LAB = nc.alloc_sbuf_tensor_at("LAB", [128, 32 * R], BF16, offset=Sbuf.BASE + 90112)
    O1S = nc.alloc_sbuf_tensor_at("O1S", [64, TOK], BF16, offset=Sbuf.BASE + 94208)
    sb.off = Sbuf.BASE + 160 * 1024
    M0 = sb.alloc("M0", [128, D // QBLK], F32)
    M1 = sb.alloc("M1", [128, D // QBLK], F32)
    M2 = sb.alloc("M2", [128, D // QBLK], F32)
    ST1 = sb.alloc("ST1", [128, 1], F32)
    ST2 = sb.alloc("ST2", [128, 1], F32)
    TMP1 = sb.alloc("TMP1", [128, 1], F32)
    RSTD = sb.alloc("RSTD", [128, 1], F32)
    NBIAS = sb.alloc("NBIAS", [128, 1], F32)
    XNB = [sb.alloc(f"XNB{i}", [128, D], BF16) for i in range(2)]
    XQB = [sb.alloc(f"XQB{i}", [128, D], BF16) for i in range(2)]
    WMSK = sb.alloc("WMSK", [128, FSH], U8)
    OSC = sb.alloc("OSC", [128, D // 128], F32)
    RAWB = sb.alloc("RAWB", [128, FSH], F32)
    IDN = sb.alloc("IDN", [128, 128], BF16)

    scr = {"A": A, "B": Bt, "C": Ct, "E": Et, "I": It,
           "M0": M0, "M1": M1, "M2": M2, "SG": SG}

    ps = [nc.alloc_psum_tensor(f"ps{i}", [128, 512], F32) for i in range(4)]
    pst_t = nc.alloc_psum_tensor("pst_t", [128, 128], BF16)

    LD, STs, VC, SC, MM, GP = "ld", "st", "vc", "sc", "mm", "gp"

    p.dma("sync", WMSK[:], wdm, LD)
    p.dma("sync", OSC[:], osc, LD)
    p.dma("sync", IDN[:], idn, LD)
    const_ld = p.val(LD)

    tiles = []
    for i in range(TOK // 128):
        tiles.append(("x", i))
    for tag in ("wg", "wu"):
        for r0 in range(0, FSH, 128):
            tiles.append((tag, r0))
    for r0 in range(0, D, 128):
        tiles.append(("wd", r0))

    ld_sem = ["lda", "ldb"]
    st_sem = ["sta", "stb"]
    ld_val = {}
    st_val = {}
    qdq_val = {}

    for ti, (kind, idx) in enumerate(tiles):
        par = ti % 2
        xa = XA[par]
        xnb = XNB[par]
        if ti >= 2:
            p.wait("sync", "vc", qdq_val[ti - 2])
        if kind == "x":
            src_ap = xp[128 * idx:128 * (idx + 1), :]
            rows, Fw = 128, D
        elif kind == "wg":
            rows = min(128, FSH - idx)
            src_ap = wgs[idx:idx + rows, :]
            Fw = D
        elif kind == "wu":
            rows = min(128, FSH - idx)
            src_ap = wus[idx:idx + rows, :]
            Fw = D
        else:
            rows, Fw = 128, FSH
            src_ap = wds[idx:idx + rows, :]
        ld_val[ti] = p.dma("sync", xa[:rows, :Fw], src_ap, ld_sem[par])
        p.wait("vector", ld_sem[par], ld_val[ti])
        p.wait("scalar", ld_sem[par], ld_val[ti])
        if ti == 0:
            p.wait("vector", LD, const_ld)
            p.wait("scalar", LD, const_ld)

        if kind == "x":
            p.op("vector", lambda e, _x=xa: e.reduce_sum(ST1[:], _x[:], AX.X))
            scv = p.op("scalar", lambda e, _x=xa: e.activation(
                XSQ[:], _x[:], AF.Square), SC)
            p.wait("vector", SC, scv)
            p.op("vector", lambda e: e.reduce_sum(ST2[:], XSQ[:], AX.X))
            p.op("vector", lambda e: e.tensor_scalar(
                ST1[:], ST1[:], 1.0 / D, None, ALU.mult))
            p.op("vector", lambda e: e.tensor_scalar(
                ST2[:], ST2[:], 1.0 / D, None, ALU.mult))
            p.op("vector", lambda e: e.tensor_tensor(
                TMP1[:], ST1[:], ST1[:], ALU.mult))
            p.op("vector", lambda e: e.tensor_tensor(
                ST2[:], ST2[:], TMP1[:], ALU.subtract))
            vcv = p.op("vector", lambda e: e.tensor_scalar(
                ST2[:], ST2[:], 1e-5, None, ALU.add))
            p.wait("scalar", "vc", vcv)
            p.op("scalar", lambda e: e.activation(RSTD[:], ST2[:], AF.Ln))
            scv = p.op("scalar", lambda e: e.activation(
                RSTD[:], RSTD[:], AF.Exp, scale=-0.5), SC)
            p.wait("vector", SC, scv)
            p.op("vector", lambda e: e.tensor_tensor(
                NBIAS[:], ST1[:], RSTD[:], ALU.mult))
            p.op("vector", lambda e: e.tensor_scalar(
                NBIAS[:], NBIAS[:], -1.0, None, ALU.mult))
            p.op("vector", lambda e, _x=xa: e.tensor_scalar(
                A[:], _x[:], RSTD[:], NBIAS[:], ALU.mult, ALU.add))
            if ti >= 2:
                p.wait("vector", st_sem[par], st_val[ti - 2])
            vcv = p.op("vector", lambda e, _n=xnb: e.tensor_copy(_n[:], A[:]))
            p.wait("scalar", "vc", vcv)
            emit_qdq(p, scr, A[:], XQB[par][:], 128, D, SC)
            vcv = p.op("vector", lambda e, _n=xnb, _q=XQB[par]: e.tensor_copy(
                _q[:, D - SEL:D], _n[:, D - SEL:D]))
            qdq_val[ti] = vcv
            p.wait("sync", "vc", vcv)
            p.dma("sync", xn_d[128 * idx:128 * (idx + 1), :], xnb[:],
                  st_sem[par])
            st_val[ti] = p.dma("sync", xq_o[128 * idx:128 * (idx + 1), :],
                               XQB[par][:], st_sem[par])
        else:
            scaled = kind == "wd"
            sc_ap = OSC[:rows, idx // 128:idx // 128 + 1] if scaled else None
            if ti >= 2:
                p.wait("vector", st_sem[par], st_val[ti - 2])
            vcv = emit_qdq(p, scr, xa[:rows, :Fw], xnb[:rows, :Fw], rows, Fw,
                           SC, scale_ap=sc_ap)
            if kind in ("wg", "wu"):
                vcv = p.op("vector", lambda e, _r=rows, _x=xa, _n=xnb:
                           e.tensor_copy(_n[:_r, D - SEL:D],
                                         _x[:_r, D - SEL:D]))
            else:
                p.op("vector", lambda e, _r=rows, _x=xa, _s=sc_ap:
                     e.tensor_scalar(RAWB[:_r, :FSH], _x[:_r, :FSH], _s,
                                     None, ALU.mult))
                vcv = p.op("vector", lambda e, _r=rows, _n=xnb:
                           e.copy_predicated(_n[:_r, :FSH], WMSK[:_r, :FSH],
                                             RAWB[:_r, :FSH]))
            qdq_val[ti] = vcv
            p.wait("sync", "vc", vcv)
            dst = {"wg": wg_o, "wu": wu_o, "wd": wd_o}[kind]
            st_val[ti] = p.dma("sync", dst[idx:idx + rows, :],
                               xnb[:rows, :Fw], st_sem[par])

    p.phase_barrier()

    for t in range(32):
        p.dma("sync", XNT[:, TOK * t:TOK * (t + 1)],
              xn_d[:, 128 * t:128 * (t + 1)], LD, transpose=True)
    xnt_ld = p.val(LD)
    p.wait("tensor", LD, xnt_ld)

    def xnt(t):
        return XNT[:, TOK * t:TOK * (t + 1)]

    lab_ld = p.dma("gpsimd", LAB[:].rearrange("p (t r) -> p t r", r=R),
                   la.rearrange("(t p) r -> p t r", p=128), "lab")

    blocks = []
    for o in range(32):
        blocks.append(("q", o))
    for o in range(8):
        blocks.append(("k", o))
    for o in range(8):
        blocks.append(("v", o))
    wsrc_map = {"q": wq, "k": wk, "v": wv}
    dst_map = {"q": qt_o, "k": kt_o, "v": v_o}
    gp_sem = ["gpa", "gpb"]
    gld = {}
    mm_done = {}
    ev_done = {}

    def load_block(bi):
        kindb, o = blocks[bi]
        par = bi % 2
        if bi >= 2:
            p.wait("gpsimd", MM, mm_done[bi - 2])
        gld[bi] = p.dma("gpsimd",
                        WB[par][:].rearrange("p (t c) -> p t c", c=128),
                        wsrc_map[kindb][:, 128 * o:128 * (o + 1)].rearrange(
                            "(t p) c -> p t c", p=128), gp_sem[par])

    load_block(0)
    load_block(1)
    for bi in range(len(blocks)):
        kindb, o = blocks[bi]
        par = bi % 2
        p.wait("tensor", gp_sem[par], gld[bi])
        if bi >= 2:
            p.wait("tensor", "vc", ev_done[bi - 2])
        for nchunk in range(2):
            pb = ps[2 * par + nchunk]
            for dt in range(32):
                mmv = p.op("tensor",
                           lambda e, _b=par, _dt=dt, _n=nchunk, _p=pb: e.matmul(
                               _p[:, 0:512],
                               WB[_b][:, 128 * _dt:128 * (_dt + 1)],
                               xnt(_dt)[:, 512 * _n:512 * (_n + 1)],
                               start=(_dt == 0), stop=(_dt == 31)),
                           MM if dt == 31 else None)
        mm_done[bi] = mmv
        if bi + 2 < len(blocks):
            load_block(bi + 2)
        p.wait("vector", MM, mmv)
        if bi >= 2:
            p.wait("vector", st_sem[par], st_val.get(("p2", bi - 2), 0))
        for nchunk in range(2):
            pb = ps[2 * par + nchunk]
            vcv = p.op("vector", lambda e, _p=pb, _n=nchunk, _q=par:
                       e.tensor_copy(QSTG[_q][:, 512 * _n:512 * (_n + 1)],
                                     _p[:, 0:512]))
        ev_done[bi] = vcv
        if kindb != "v":
            p.wait("sync", "vc", vcv)
            st_val[("p2", bi)] = p.dma(
                "sync", dst_map[kindb][128 * o:128 * (o + 1), :],
                QSTG[par][:], st_sem[par])
        else:
            p.wait("tensor", "vc", ev_done[bi])
            for tt in range(8):
                mmv2 = p.op("tensor", lambda e, _tt=tt, _q=par: e.transpose(
                    pst_t[:], QSTG[_q][:, 128 * _tt:128 * (_tt + 1)], IDN[:]),
                    MM)
                p.wait("vector", MM, mmv2)
                vcv = p.op("vector", lambda e, _tt=tt, _q=par: e.tensor_copy(
                    VSTG[_q][:, 128 * _tt:128 * (_tt + 1)], pst_t[:]))
                p.wait("tensor", "vc", vcv)
            p.wait("sync", "vc", vcv)
            for tt in range(8):
                st_val[("p2", bi)] = p.dma(
                    "sync", dst_map[kindb][128 * tt:128 * (tt + 1),
                                           128 * o:128 * (o + 1)],
                    VSTG[par][:, 128 * tt:128 * (tt + 1)], st_sem[par])
            mm_done[bi] = mmv2

    p.wait("tensor", "lab", lab_ld)
    p.wait("tensor", "vc", ev_done[len(blocks) - 1])
    for nchunk in range(2):
        pb = ps[2 + nchunk]
        for dt in range(32):
            mmv = p.op("tensor", lambda e, _dt=dt, _n=nchunk, _p=pb: e.matmul(
                _p[0:64, 0:512], LAB[:, R * _dt:R * (_dt + 1)],
                xnt(_dt)[:, 512 * _n:512 * (_n + 1)],
                start=(_dt == 0), stop=(_dt == 31)),
                MM if dt == 31 else None)
        p.wait("vector", MM, mmv)
        vcv = p.op("vector", lambda e, _p=pb, _n=nchunk: e.tensor_copy(
            O1S[0:64, 512 * _n:512 * (_n + 1)], _p[0:64, 0:512]))
    p.wait("sync", "vc", vcv)
    p.dma("sync", o1t_o, O1S[0:64, :], STs)

    p.phase_barrier(("sync",))

    sems = {}
    stack = contextlib.ExitStack()
    for name in (LD, STs, VC, SC, MM, GP, "gpc", "lda", "ldb",
                 "sta", "stb", "gpa", "gpb", "lab"):
        sems[name] = stack.enter_context(nc.semaphore(name))
    with nc.Block() as block:
        p.replay(block, sems)
    stack.close()
    return nc



def _launch1_inputs(inputs):
    X = np.asarray(inputs["L0_attn_hidden_states"]).reshape(NB * S, D)
    up = np.asarray(inputs["L0_ffn_up_reorder_index"]).astype(np.int64)
    dn = np.asarray(inputs["L0_ffn_down_reorder_index"]).astype(np.int64)
    PR = _row_perm()

    Xp = np.ascontiguousarray(X[PR][:, up])
    WQ = np.ascontiguousarray(np.asarray(inputs["L0_attn_w_q"])[up, :])
    WK = np.ascontiguousarray(np.asarray(inputs["L0_attn_w_k"])[up, :])
    WV = np.ascontiguousarray(np.asarray(inputs["L0_attn_w_v"])[up, :])
    LA = np.ascontiguousarray(np.asarray(inputs["L0_ffn_lr_A"])[up, :])
    WGp = np.asarray(inputs["L0_ffn_w_gate"])[dn][:, up]
    WUp = np.asarray(inputs["L0_ffn_w_up"])[dn][:, up]
    WDp = np.asarray(inputs["L0_ffn_w_down"])[:, dn]
    oscale = np.asarray(inputs["L0_ffn_mlp_output_scale"]).astype(np.float32)
    osc = np.ascontiguousarray(oscale.reshape(D // 128, 128).T)
    idn = np.eye(128, dtype=BF)

    in_maps = []
    for c in range(NCORES):
        f0 = FSH * c
        keep = (np.arange(f0, f0 + FSH) >= FF - SEL).astype(np.uint8)
        wdmask = np.ascontiguousarray(np.broadcast_to(keep, (128, FSH)))
        in_maps.append({
            "xp": np.ascontiguousarray(Xp[TOK * c:TOK * (c + 1)]),
            "wq": WQ, "wk": WK, "wv": WV, "la": LA,
            "wgs": np.ascontiguousarray(WGp[f0:f0 + FSH]),
            "wus": np.ascontiguousarray(WUp[f0:f0 + FSH]),
            "wds": np.ascontiguousarray(WDp[:, f0:f0 + FSH]),
            "wdm": wdmask, "osc": osc, "idn": idn,
        })
    return in_maps, PR



FCH = 512
NFCH = 22
CW = [512] * 21 + [256]
FT = FF // 128
FHALF = FT // 2


def build_launch2():
    nc = bass.Bass("TRN2", target_bir_lowering=False)
    sb = Sbuf(nc)
    p = Prog()

    xres = nc.dram_tensor("xres", [TOK, D], F32, kind="ExternalInput").ap()
    xqt = nc.dram_tensor("xqt", [128, 32 * TOK], BF16,
                         kind="ExternalInput").ap()
    qt = nc.dram_tensor("qt", [DQ, TOK], BF16, kind="ExternalInput").ap()
    ktb = nc.dram_tensor("ktb", [DKV, S], BF16, kind="ExternalInput").ap()
    vaug = nc.dram_tensor("vaug", [S, NKV * 129], BF16, kind="ExternalInput").ap()
    o1t = nc.dram_tensor("o1t", [R, TOK], BF16, kind="ExternalInput").ap()
    lrb = nc.dram_tensor("lrb", [R, D], F32, kind="ExternalInput").ap()
    wo = nc.dram_tensor("wo", [8 * 128, 32 * 512], F32,
                        kind="ExternalInput").ap()
    wgt = nc.dram_tensor("wgt", [NFCH * 128, 32 * FCH], BF16,
                         kind="ExternalInput").ap()
    wut = nc.dram_tensor("wut", [NFCH * 128, 32 * FCH], BF16,
                         kind="ExternalInput").ap()
    wdt = nc.dram_tensor("wdt", [16 * 128, FHALF * 512], BF16,
                         kind="ExternalInput").ap()
    mskr = nc.dram_tensor("mskr", [8, 2, 128, 128], BF16, kind="ExternalInput").ap()
    ucst = nc.dram_tensor("ucst", [128, 128], BF16, kind="ExternalInput").ap()
    one1 = nc.dram_tensor("one1", [1, 128], BF16, kind="ExternalInput").ap()
    onec = nc.dram_tensor("onec", [128, 1], BF16, kind="ExternalInput").ap()
    idn = nc.dram_tensor("idn", [128, 128], BF16, kind="ExternalInput").ap()

    out = nc.dram_tensor("out", [TOK, D], F32, kind="ExternalOutput").ap()

    attnt_d = nc.dram_tensor("attnt_d", [128, 32 * TOK], BF16).ap()
    hqt_d = nc.dram_tensor("hqt_d", [FF, TOK], BF16).ap()
    down_d = nc.dram_tensor("down_d", [TOK, D], F32).ap()

    OV = Sbuf.BASE

    def ov(name, shape, dtype, off):
        return nc.alloc_sbuf_tensor_at(name, list(shape), dtype, offset=OV + off)

    KTS = ov("KTS", [128, NKV * S], BF16, 0)
    VAS = ov("VAS", [128, 16 * 1032], BF16, 32768)
    QTS = ov("QTS", [128, 32 * TOK], BF16, 65792)
    KSQ = ov("KSQ", [128, S], BF16, 131328)
    PT = [ov(f"PT{i}", [128, 512], BF16, 135424 + 1024 * i) for i in range(2)]
    ATL = ov("ATL", [128, 128], BF16, 137472)
    ATLT = ov("ATLT", [128, 128], BF16, 137728)
    MSK = ov("MSK", [128, 8 * 2 * 128], BF16, 137984)
    XQS = ov("XQS", [128, 32 * TOK], BF16, 0)
    WGC = [ov(f"WGC{i}", [128, 32 * FCH], BF16, 65536 + 32768 * i)
           for i in range(2)]
    WUC = ov("WUC", [128, 32 * FCH], BF16, 131072)
    QS = {}
    boff = 163840
    for nm, dt_ in (("A", F32), ("B", F32), ("C", F32), ("E", F32),
                    ("I", I32), ("SG", F32)):
        QS[nm] = ov("q" + nm, [128, FCH], dt_, boff)
        boff += FCH * 4
    H_ = ov("H_", [128, FCH], F32, boff); boff += FCH * 4
    SILB = ov("SILB", [128, 8 * FCH], F32, boff); boff += 8 * FCH * 4
    HQB = [None] * 4
    for _ti in range(4):
        HQB[_ti] = ov(f"HQ{_ti}", [128, FCH], BF16, boff); boff += FCH * 2
    HQTW2 = ov("HQW2", [128, 4 * TOK], BF16, boff)
    boff += 4 * TOK * 2
    WDC = [ov("WDCa", [128, FHALF * 512], BF16, 0),
           ov("WDCb", [128, FHALF * 512], BF16, 69888)]
    HQF = [ov(f"HQF{i}", [128, TOK], BF16, 45056 + 2048 * i)
           for i in range(4)]
    DSTG = [ov(f"DSTG{i}", [128, 512], F32, 53248 + 2048 * i)
            for i in range(2)]
    ATT2F = ov("ATT2F", [128, 32 * TOK], BF16, 113920)
    WOC = [ov(f"WOC{i}", [128, 32 * 512], BF16, 32768 * i)
           for i in range(2)]
    LRB = ov("LRB", [64, D], BF16, 69888)
    DT2 = [ov(f"DT{i}", [128, 512], F32, 78080 + 2048 * i) for i in range(2)]
    XT2 = [ov(f"XT{i}", [128, 512], F32, 82176 + 2048 * i) for i in range(2)]
    OS2 = [ov(f"OS{i}", [128, 512], F32, 86272 + 2048 * i) for i in range(2)]

    sb.off = Sbuf.BASE + 206848
    UC = sb.alloc("UC", [128, 128], BF16)
    IDN = sb.alloc("IDN", [128, 128], BF16)
    O1S = sb.alloc("O1S", [64, TOK], BF16)
    ONE1 = sb.alloc("ONE1", [1, 128], BF16)
    ONEC = sb.alloc("ONEC", [128, 1], BF16)
    KN2 = sb.alloc("KN2", [1, NKV], F32)
    QN2 = sb.alloc("QN2", [1, 32], F32)
    TR1 = sb.alloc("TR1", [1, 1], F32)
    TR2 = sb.alloc("TR2", [1, 1], F32)
    BNDB = sb.alloc("BNDB", [1, 32], BF16)
    BIASC = sb.alloc("BIASC", [128, 32], F32)
    RZ = sb.alloc("RZ", [128, 1], F32)
    M0 = sb.alloc("M0", [128, FCH // QBLK], F32)
    M1 = sb.alloc("M1", [128, FCH // QBLK], F32)
    M2 = sb.alloc("M2", [128, FCH // QBLK], F32)
    QS["M0"], QS["M1"], QS["M2"] = M0, M1, M2

    ps = [nc.alloc_psum_tensor(f"ps{i}", [128, 512], F32) for i in range(8)]
    psT = ps[6][:, 0:64].bitcast(BF16)

    LD, STs, MM, GP = "ld", "st", "mm", "gp"

    p.dma("sync", KTS[:].rearrange("p (h k) -> p h k", k=S),
          ktb.rearrange("(h p) k -> p h k", p=128), LD)
    p.dma("sync", VAS[:].rearrange("p (t c) -> p t c", c=1032),
          vaug.rearrange("(t p) c -> p t c", p=128), LD)
    p.dma("sync", QTS[:].rearrange("p (h q) -> p h q", q=TOK),
          qt.rearrange("(h p) q -> p h q", p=128), LD)
    p.dma("sync", MSK[:].rearrange("p (s j q) -> p s j q", j=2, q=128),
          mskr.rearrange("s j p q -> p s j q"), LD)
    p.dma("sync", UC[:], ucst, LD)
    p.dma("sync", IDN[:], idn, LD)
    p.dma("sync", O1S[0:64, :], o1t, LD)
    p.dma("sync", ONE1[0:1, :], one1, LD)
    ldv = p.dma("sync", ONEC[:], onec, LD)
    p.wait("scalar", LD, ldv)
    p.wait("tensor", LD, ldv)
    p.wait("vector", LD, ldv)

    def norms_for(block_ap, width, out_row, out_col):
        p.wait("scalar", MM, p.val(MM))
        scv = p.op("scalar", lambda e, _b=block_ap, _w=width: e.activation(
            KSQ[:, 0:_w], _b, AF.Square))
        p.wait("tensor", "sc", scv)
        for cidx in range(width // 512):
            mmv = p.op("tensor", lambda e, _c=cidx: e.matmul(
                ps[7][0:1, 0:512], ONEC[:],
                KSQ[:, 512 * _c:512 * (_c + 1)], start=True, stop=True), MM)
            p.wait("vector", MM, mmv)
            if cidx == 0:
                vcv = p.op("vector", lambda e: e.tensor_reduce(
                    TR1[0:1, :], ps[7][0:1, 0:512], AX.X, ALU.max))
            else:
                p.op("vector", lambda e: e.tensor_reduce(
                    TR2[0:1, :], ps[7][0:1, 0:512], AX.X, ALU.max))
                vcv = p.op("vector", lambda e: e.tensor_tensor(
                    TR1[0:1, :], TR1[0:1, :], TR2[0:1, :], ALU.max))
            p.wait("tensor", "vc", vcv)
        p.op("vector", lambda e, _o=out_col, _r=out_row: e.tensor_copy(
            _r[0:1, _o:_o + 1], TR1[0:1, :]))

    for h in range(NKV):
        norms_for(KTS[:, S * h:S * (h + 1)], S, KN2, h)
    for H in range(32):
        norms_for(QTS[:, TOK * H:TOK * (H + 1)], TOK, QN2, H)

    for kvh in range(NKV):
        p.op("vector", lambda e, _h=kvh: e.tensor_reduce(
            TR1[0:1, :], QN2[0:1, 4 * _h:4 * (_h + 1)], AX.X, ALU.max))
        p.op("vector", lambda e, _h=kvh: e.tensor_tensor(
            TR1[0:1, :], TR1[0:1, :], KN2[0:1, _h:_h + 1], ALU.add))
        vcv = p.op("vector", lambda e, _h=kvh: e.tensor_scalar(
            BNDB[0:1, _h:_h + 1], TR1[0:1, :], -SCALE / 2.0, None, ALU.mult))
        p.wait("tensor", "vc", vcv)
        mmv = p.op("tensor", lambda e, _h=kvh: e.matmul(
            ps[7][:, 0:1], ONE1[0:1, :], BNDB[0:1, _h:_h + 1],
            start=True, stop=True), MM)
        p.wait("vector", MM, mmv)
        vcv = p.op("vector", lambda e, _h=kvh: e.tensor_copy(
            BIASC[:, _h:_h + 1], ps[7][:, 0:1]))
        p.wait("tensor", "vc", vcv)

    for kvh in range(NKV):
        for slot in range(8):
            E = EXT[slot]
            exp_done = [None] * E

            def sc_stage2(kt):
                pa = ps[kt % 2]
                if kt >= 2:
                    p.wait("tensor", "sc", exp_done[kt - 2])
                lm = kt >= E - 2
                last_mm = None
                for qh in range(4):
                    H = 4 * kvh + qh
                    is_last = (qh == 3) and not lm
                    last_mm = p.op("tensor",
                                   lambda e, _kvh=kvh, _H=H, _s=slot, _kt=kt,
                                   _pa=pa, _lm=lm, _qh=qh: e.matmul(
                                       _pa[:, 128 * _qh:128 * (_qh + 1)],
                                       KTS[:, S * _kvh + 128 * _kt:
                                           S * _kvh + 128 * (_kt + 1)],
                                       QTS[:, TOK * _H + 128 * _s:
                                           TOK * _H + 128 * (_s + 1)],
                                       start=(_qh == 0),
                                       stop=(_qh == 3 and not _lm)),
                                   MM if is_last else None)
                    if lm:
                        j = kt - (E - 2)
                        last_mm = p.op("tensor",
                                       lambda e, _pa=pa, _s=slot, _j=j,
                                       _qh=qh: e.matmul(
                                           _pa[:, 128 * _qh:128 * (_qh + 1)],
                                           UC[:],
                                           MSK[:, (2 * _s + _j) * 128:
                                               (2 * _s + _j + 1) * 128],
                                           start=False, stop=(_qh == 3)),
                                       MM if qh == 3 else None)
                return last_mm

            def exp_stage(kt, mmv):
                pa = ps[kt % 2]
                p.wait("scalar", MM, mmv)
                ex = p.op("scalar", lambda e, _pa=pa, _k=kt, _h=kvh: e.activation(
                    PT[_k % 2][:, 0:512], _pa[:, 0:512], AF.Exp,
                    scale=SCALE, bias=BIASC[:, _h:_h + 1]))
                exp_done[kt] = ex

            def av_stage(kt):
                pat0 = 2
                p.wait("tensor", "sc", exp_done[kt])
                lastv = None
                for qh in range(4):
                    lastv = p.op("tensor",
                                 lambda e, _k=kt, _kvh=kvh, _qh=qh, _E=E: e.matmul(
                                     ps[2 + _qh][:, 0:129], PT[_k % 2][:, 128 * _qh:128 * (_qh + 1)],
                                     VAS[:, 1032 * _k + 129 * _kvh:
                                         1032 * _k + 129 * (_kvh + 1)],
                                     start=(_k == 0), stop=(_k == _E - 1)),
                                 MM if (kt == E - 1 and qh == 3) else None)
                return lastv

            mm_prev = sc_stage2(0)
            exp_stage(0, mm_prev)
            for kt in range(1, E):
                mm_prev = sc_stage2(kt)
                av_stage(kt - 1)
                exp_stage(kt, mm_prev)
            av_last = av_stage(E - 1)

            p.wait("vector", MM, av_last)
            for qh in range(4):
                H = 4 * kvh + qh
                p.op("vector", lambda e, _q=qh: e.reciprocal(
                    RZ[:], ps[2 + _q][:, 128:129]))
                vcv = p.op("vector", lambda e, _q=qh: e.tensor_scalar(
                    ATL[:], ps[2 + _q][:, 0:128], RZ[:], None, ALU.mult))
                p.wait("tensor", "vc", vcv)
                mmv = p.op("tensor", lambda e: e.transpose(
                    psT[:, 0:128], ATL[:], IDN[:]), MM)
                p.wait("vector", MM, mmv)
                vcv = p.op("vector", lambda e: e.tensor_copy(
                    ATLT[:], psT[:, 0:128]))
                p.wait("tensor", "vc", vcv)
                p.wait("sync", "vc", vcv)
                stv = p.dma("sync",
                            attnt_d[:, TOK * H + 128 * slot:
                                    TOK * H + 128 * (slot + 1)],
                            ATLT[:], STs)
                p.wait("vector", STs, stv)

    p.phase_barrier()

    xq_ld = p.dma("sync", XQS[:], xqt, "xq")
    g_ld, u_ld = {}, {}
    g_mm, u_mm = {}, {}
    silu_done, hev, qdq_done = {}, {}, {}
    tdone, tcopy_done, st_done = {}, {}, {}
    h_sc_guard = {}

    def load_g(ch):
        buf = ch % 2
        if ch >= 2:
            p.wait("sync", MM, g_mm[(ch - 2, 7)])
        g_ld[ch] = p.dma("sync", WGC[buf][:],
                         wgt[128 * ch:128 * (ch + 1), :],
                         "wga" if buf == 0 else "wgb")

    def load_u(ch):
        if ch >= 1:
            p.wait("sync", MM, u_mm[(ch - 1, 7)])
        u_ld[ch] = p.dma("sync", WUC[:], wut[128 * ch:128 * (ch + 1), :],
                         "wua" if ch % 2 == 0 else "wub")

    def trans_block(ch, tt):
        w = CW[ch]
        nj = w // 128
        hq = HQB[tt % 4]
        p.wait("tensor", "vc", qdq_done[(ch, tt)])
        if tt == 0 and ch >= 1:
            p.wait("scalar", *st_done[ch - 1])
        for j in range(nj):
            mmv = p.op("tensor", lambda e, _j=j, _h=hq, _tt=tt, _nj=nj:
                       e.matmul(
                           ps[_tt][:, 64 * _j:64 * (_j + 1)].bitcast(BF16),
                           _h[:, 128 * _j:128 * (_j + 1)], IDN[:],
                           start=(_j == 0), stop=(_j == _nj - 1),
                           is_transpose=True),
                       MM if j == nj - 1 else None)
        tdone[(ch, tt)] = mmv
        p.wait("scalar", MM, mmv)
        scv = p.op("scalar", lambda e, _tt=tt, _nj=nj: e.activation(
            HQTW2[:].rearrange("p (j t q) -> p j t q", j=4, q=128)
            [:, 0:_nj, _tt, :],
            ps[_tt][:, 0:64 * _nj].bitcast(BF16).rearrange(
                "p (j q) -> p j q", q=128), AF.Copy))
        tcopy_done[(ch, tt)] = scv
        if tt == 7:
            p.wait("sync", "sc", scv)
            ssem = "sta" if ch % 2 == 0 else "stb"
            for j in range(nj):
                p.dma("sync",
                      hqt_d[FCH * ch + 128 * j:FCH * ch + 128 * (j + 1), :],
                      HQTW2[:, TOK * j:TOK * (j + 1)], ssem)
            st_done[ch] = (ssem, p.val(ssem))

    def g_block(ch, tt):
        buf = ch % 2
        w = CW[ch]
        if ch > 0 and tt >= 5:
            trans_block(ch - 1, tt)
        if tt == 0:
            p.wait("tensor", "wga" if buf == 0 else "wgb", g_ld[ch])
            if ch == 0:
                p.wait("tensor", "xq", xq_ld)
        if ch > 0:
            p.wait("tensor", "sc", tcopy_done[(ch - 1, tt)])
        for dt in range(32):
            mmv = p.op("tensor", lambda e, _dt=dt, _tt=tt, _b=buf, _w=w:
                       e.matmul(
                           ps[_tt][:, 0:_w],
                           XQS[:, TOK * _dt + 128 * _tt:
                               TOK * _dt + 128 * (_tt + 1)],
                           WGC[_b][:, FCH * _dt:FCH * _dt + _w],
                           start=(_dt == 0), stop=(_dt == 31)),
                       MM if dt == 31 else None)
        g_mm[(ch, tt)] = mmv
        p.wait("scalar", MM, mmv)
        if ch > 0:
            p.wait("scalar", "vc", hev[(ch - 1, tt)])
        silu_done[(ch, tt)] = p.op(
            "scalar", lambda e, _tt=tt, _w=w: e.activation(
                SILB[:, FCH * _tt:FCH * _tt + _w], ps[_tt][:, 0:_w],
                AF.Silu))

    def u_block(ch, tt):
        w = CW[ch]
        if tt >= 3:
            trans_block(ch, tt - 3)
        if tt == 0:
            p.wait("tensor", "wua" if ch % 2 == 0 else "wub", u_ld[ch])
        p.wait("tensor", "sc", silu_done[(ch, tt)])
        for dt in range(32):
            mmv = p.op("tensor", lambda e, _dt=dt, _tt=tt, _w=w: e.matmul(
                ps[_tt][:, 0:_w],
                XQS[:, TOK * _dt + 128 * _tt:TOK * _dt + 128 * (_tt + 1)],
                WUC[:, FCH * _dt:FCH * _dt + _w],
                start=(_dt == 0), stop=(_dt == 31)),
                MM if dt == 31 else None)
        u_mm[(ch, tt)] = mmv
        p.wait("vector", MM, mmv)
        prevk = (ch, tt - 4) if tt >= 4 else (ch - 1, tt + 4)
        if prevk in tdone:
            p.wait("vector", MM, tdone[prevk])
        p.wait("vector", "sc", p.val("sc"))
        vcv = p.op("vector", lambda e, _tt=tt, _w=w: e.tensor_tensor(
            H_[:, 0:_w], SILB[:, FCH * _tt:FCH * _tt + _w],
            ps[_tt][:, 0:_w], ALU.mult))
        hev[(ch, tt)] = vcv
        p.wait("scalar", "vc", vcv)
        emit_qdq(p, QS, H_[:, 0:w], HQB[tt % 4][:, 0:w], 128, w, "sc")
        if ch == NFCH - 1:
            p.op("vector", lambda e, _tt=tt, _w=w: e.tensor_copy(
                HQB[_tt % 4][:, _w - SEL:_w], H_[:, _w - SEL:_w]))
        qdq_done[(ch, tt)] = p.op("vector", lambda e: e.engine_nop())

    load_g(0)
    load_u(0)
    load_g(1)
    for ch in range(NFCH):
        for tt in range(8):
            g_block(ch, tt)
        if ch + 2 < NFCH:
            load_g(ch + 2)
        for tt in range(8):
            u_block(ch, tt)
        if ch + 1 < NFCH:
            load_u(ch + 1)
    for tt in range(5, 8):
        trans_block(NFCH - 1, tt)

    p.phase_barrier()

    att_ld = p.dma("gpsimd", ATT2F[:], attnt_d, "gat")

    evac_c1 = {}
    mm_c1 = {}
    wd_ld = {}
    wd_sem = ["wd", "we"]

    def c1_load_wdc(sl):
        par = sl % 2
        if sl >= 2:
            p.wait("sync", MM, mm_c1.get(sl - 2, 0))
        wd_ld[sl] = p.dma("sync", WDC[par][:],
                          wdt[128 * sl:128 * (sl + 1), :], wd_sem[par])

    c1_load_wdc(0)
    c1_load_wdc(1)
    hq_sems = ["hqa", "hqb", "hqc", "hqd"]
    hq_ld = {}
    hq_rd = {}

    def c1_load_hq(gi):
        b = gi % 4
        sl_, ft_ = divmod(gi, FHALF)
        fh_ = sl_ % 2
        prev = hq_rd.get(gi - 4)
        if prev:
            p.wait("sync", MM, prev)
        r0 = FHALF * 128 * fh_ + 128 * ft_
        hq_ld[gi] = p.dma("sync", HQF[b][:], hqt_d[r0:r0 + 128, :],
                          hq_sems[b])

    for pre in range(4):
        c1_load_hq(pre)
    ds_sems = ["dsa", "dsb"]
    ds_val = {}
    for dc in range(8):
        for fh in range(2):
            sl = dc * 2 + fh
            par = sl % 2
            for ft in range(FHALF):
                gi = sl * FHALF + ft
                b = gi % 4
                p.wait("tensor", hq_sems[b], hq_ld[gi])
                if ft == 0:
                    p.wait("tensor", wd_sem[par], wd_ld[sl])
                for tt in range(8):
                    if fh == 0 and ft == 0 and dc > 0:
                        p.wait("tensor", "vc", evac_c1[(dc - 1, tt)])
                    mmv = p.op("tensor",
                               lambda e, _ft=ft, _fh=fh, _b=b, _tt=tt,
                               _par=par: e.matmul(
                                   ps[_tt][:, 0:512],
                                   HQF[_b][:, 128 * _tt:128 * (_tt + 1)],
                                   WDC[_par][:, 512 * _ft:512 * (_ft + 1)],
                                   start=(_ft == 0 and _fh == 0),
                                   stop=(_ft == FHALF - 1 and _fh == 1)),
                               MM if tt == 7 else None)
                hq_rd[gi] = mmv
                mm_c1[sl] = mmv
                if gi + 4 < 16 * FHALF:
                    c1_load_hq(gi + 4)
            if sl + 2 < 16:
                c1_load_wdc(sl + 2)
        p.wait("vector", MM, mm_c1[dc * 2 + 1])
        for tt in range(8):
            dpar = tt % 2
            pv = ds_val.get((dc, tt - 2), ds_val.get((dc - 1, tt + 6)))
            if pv is not None:
                p.wait("vector", ds_sems[dpar], pv)
            vcv = p.op("vector", lambda e, _tt=tt, _dp=dpar: e.tensor_copy(
                DSTG[_dp][:], ps[_tt][:, 0:512]))
            evac_c1[(dc, tt)] = vcv
            p.wait("sync", "vc", vcv)
            ds_val[(dc, tt)] = p.dma(
                "sync", down_d[128 * tt:128 * (tt + 1),
                               512 * dc:512 * (dc + 1)],
                DSTG[dpar][:], ds_sems[dpar])

    p.phase_barrier()

    lr_ld = p.dma("gpsimd", LRB[0:64, :], lrb, "glr")
    ost_val = {}
    dt_sem = ["dtm", "dtn"]
    xt_sem = ["xt", "xu"]
    c2_ld = {}
    c2_mm = {}
    c2_ev = {}

    def c2_load(gi):
        dc, tt = divmod(gi, 8)
        par = gi % 2
        prev_ev = c2_ev.get(gi - 2)
        if prev_ev:
            p.wait("sync", "vc", prev_ev)
        d = p.dma("sync", DT2[par][:], down_d[128 * tt:128 * (tt + 1),
                                              512 * dc:512 * (dc + 1)],
                  dt_sem[par])
        x = p.dma("sync", XT2[par][:], xres[128 * tt:128 * (tt + 1),
                                            512 * dc:512 * (dc + 1)],
                  xt_sem[par])
        c2_ld[gi] = (d, x)

    gw_sem = ["gwa", "gwb"]
    for dc in range(8):
        b = dc % 2
        if dc >= 2:
            p.wait("gpsimd", MM, c2_mm[(dc - 2) * 8 + 7])
        gv = p.dma("gpsimd", WOC[b][:],
                   wo[128 * dc:128 * (dc + 1), :], gw_sem[b])
        if dc == 0:
            c2_load(0)
            c2_load(1)
        p.wait("tensor", gw_sem[b], gv)
        if dc == 0:
            p.wait("tensor", "glr", lr_ld)
            p.wait("tensor", "gat", att_ld)
        for tt in range(8):
            gi = dc * 8 + tt
            par = gi % 2
            d, x = c2_ld[gi]
            ev_prev = c2_ev.get(gi - 2)
            if ev_prev:
                p.wait("tensor", "vc", ev_prev)
            for dt in range(32):
                p.op("tensor", lambda e, _dt=dt, _b=b, _tt=tt, _par=par:
                     e.matmul(
                         ps[_par][:, 0:512],
                         ATT2F[:, TOK * _dt + 128 * _tt:
                               TOK * _dt + 128 * (_tt + 1)],
                         WOC[_b][:, 512 * _dt:512 * (_dt + 1)],
                         start=(_dt == 0), stop=False))
            mmv = p.op("tensor", lambda e, _b=b, _tt=tt, _dc=dc, _par=par: e.matmul(
                ps[_par][:, 0:512], O1S[0:64, 128 * _tt:128 * (_tt + 1)],
                LRB[0:64, 512 * _dc:512 * (_dc + 1)],
                start=False, stop=True), MM)
            c2_mm[gi] = mmv
            p.wait("vector", MM, mmv)
            p.wait("vector", dt_sem[par], d)
            p.wait("vector", xt_sem[par], x)
            stp = c2_ev.get(gi - 2)
            if stp is not None:
                p.wait("vector", "ost" if (gi - 2) % 2 == 0 else "osu",
                       ost_val[gi - 2])
            p.op("vector", lambda e, _par=par: e.tensor_tensor(
                OS2[_par][:], ps[_par][:, 0:512], DT2[_par][:], ALU.add))
            vcv = p.op("vector", lambda e, _par=par: e.tensor_tensor(
                OS2[_par][:], OS2[_par][:], XT2[_par][:], ALU.add))
            c2_ev[gi] = vcv
            p.wait("sync", "vc", vcv)
            ost_val[gi] = p.dma("sync", out[128 * tt:128 * (tt + 1),
                                            512 * dc:512 * (dc + 1)],
                                OS2[par][:], "ost" if gi % 2 == 0 else "osu")
            if gi + 2 < 64:
                c2_load(gi + 2)

    p.wait("sync", "ost", p.val("ost"))
    p.wait("sync", "osu", p.val("osu"))

    sems = {}
    stack = contextlib.ExitStack()
    for name in (LD, STs, "vc", "sc", MM, GP, "gpc", "xq", "wga", "wgb",
                 "wua", "wub",
                 "sta", "stb", "wd", "we", "hqa", "hqb", "hqc", "hqd",
                 "dsa", "dsb", "gat", "dtm",
                 "dtn", "xt", "xu", "ost", "osu", "glr", "gwa", "gwb"):
        sems[name] = stack.enter_context(nc.semaphore(name))
    with nc.Block() as block:
        p.replay(block, sems)
    stack.close()
    return nc


def _launch2_inputs(inputs, res1, PR):
    X = np.asarray(inputs["L0_attn_hidden_states"]).reshape(NB * S, D)
    up = np.asarray(inputs["L0_ffn_up_reorder_index"]).astype(np.int64)
    dn = np.asarray(inputs["L0_ffn_down_reorder_index"]).astype(np.int64)

    wg_full = np.concatenate([res1[c]["wg_o"] for c in range(NCORES)], axis=0)
    wu_full = np.concatenate([res1[c]["wu_o"] for c in range(NCORES)], axis=0)
    wd_full = np.concatenate([res1[c]["wd_o"] for c in range(NCORES)], axis=1)
    WGT = wg_full.T
    WUT = wu_full.T
    WDT = wd_full.T
    def _blk_gu(W):
        W2 = np.zeros((NFCH, 128, 32, FCH), dtype=BF)
        for ch in range(NFCH):
            w = CW[ch]
            blk = W[:, FCH * ch:FCH * ch + w].reshape(32, 128, w)
            W2[ch, :, :, :w] = blk.transpose(1, 0, 2)
        return np.ascontiguousarray(W2.reshape(NFCH * 128, 32 * FCH))

    WGT2 = _blk_gu(WGT)
    WUT2 = _blk_gu(WUT)
    WDT2 = np.ascontiguousarray(
        WDT.reshape(2, FHALF, 128, 8, 512).transpose(3, 0, 2, 1, 4)
        .reshape(16 * 128, FHALF * 512))

    ucst = (np.arange(128)[None, :] >= np.arange(128)[:, None]).astype(BF)
    one1 = np.ones((1, 128), dtype=BF)
    onec = np.ones((128, 1), dtype=BF)
    idn = np.eye(128, dtype=BF)
    wo = np.asarray(inputs["L0_attn_w_o"], np.float32)
    WO2 = np.ascontiguousarray(
        wo.reshape(32, 128, 8, 512).transpose(2, 1, 0, 3)
        .reshape(8 * 128, 32 * 512))
    lrbi = np.ascontiguousarray(np.asarray(inputs["L0_ffn_lr_B"], np.float32))

    in_maps = []
    for c in range(NCORES):
        b, par = c // 2, c % 2
        slots = SLOT_A if par == 0 else SLOT_B
        pair = [2 * b, 2 * b + 1]
        pr_pair = PR[TOK * pair[0]:TOK * (pair[0] + 1)].tolist() +                   PR[TOK * pair[1]:TOK * (pair[1] + 1)].tolist()
        rel = np.asarray(pr_pair, dtype=np.int64) - 2048 * b
        inv = np.empty(S, dtype=np.int64)
        inv[rel] = np.arange(S)
        kt_cat = np.concatenate([res1[pair[0]]["kt_o"],
                                 res1[pair[1]]["kt_o"]], axis=1)
        ktb = np.ascontiguousarray(kt_cat[:, inv])
        v_cat = np.concatenate([res1[pair[0]]["v_o"],
                                res1[pair[1]]["v_o"]], axis=0)
        v_nat = v_cat[inv, :]
        vaug = np.empty((S, NKV * 129), dtype=BF)
        for h in range(NKV):
            vaug[:, 129 * h:129 * h + 128] = v_nat[:, 128 * h:128 * (h + 1)]
            vaug[:, 129 * h + 128] = np.ones(S, dtype=BF)
        mskr = np.zeros((8, 2, 128, 128), dtype=np.float32)
        for si, t0 in enumerate(slots):
            E = EXT[si]
            for j in range(2):
                kt = E - 2 + j
                off = 128 * (t0 - kt)
                for q in range(128):
                    thr = q + off + 1
                    if thr <= 0:
                        mskr[si, j, 0, q] = -BIGM
                    elif thr <= 127:
                        mskr[si, j, thr, q] = -BIGM
        xqt2 = np.ascontiguousarray(
            res1[c]["xq_o"].T.reshape(32, 128, TOK).transpose(1, 0, 2)
            .reshape(128, 32 * TOK))
        in_maps.append({
            "xres": np.ascontiguousarray(X[PR[TOK * c:TOK * (c + 1)]]),
            "xqt": xqt2,
            "qt": res1[c]["qt_o"],
            "ktb": ktb,
            "vaug": vaug,
            "o1t": res1[c]["o1t_o"],
            "lrb": lrbi,
            "wo": WO2,
            "wgt": WGT2, "wut": WUT2, "wdt": WDT2,
            "mskr": mskr.astype(BF),
            "ucst": ucst, "one1": one1, "onec": onec, "idn": idn,
        })
    return in_maps


def kernel(**inputs):
    in_maps, PR = _launch1_inputs(inputs)
    nc1 = build_launch1()
    res1o = run_bass_kernel_spmd(nc1, in_maps, core_ids=list(range(NCORES)))
    res1 = res1o.results
    in_maps2 = _launch2_inputs(inputs, res1, PR)
    nc2 = build_launch2()
    res2 = run_bass_kernel_spmd(nc2, in_maps2, core_ids=list(range(NCORES)))
    out = np.empty((NB * S, D), dtype=np.float32)
    for c in range(NCORES):
        out[PR[TOK * c:TOK * (c + 1)]] = res2.results[c]["out"]
    return out.reshape(NB, S, D)


# version 10
# speedup vs baseline: 1.2931x; 1.0164x over previous
import contextlib
import math
import numpy as np
import ml_dtypes
import concourse.bass as bass
import concourse.mybir as mybir
from concourse.bass_utils import run_bass_kernel_spmd

F32 = mybir.dt.float32
I32 = mybir.dt.int32
BF16 = mybir.dt.bfloat16
U8 = mybir.dt.uint8
ALU = mybir.AluOpType
AF = mybir.ActivationFunctionType
AX = mybir.AxisListType
BF = ml_dtypes.bfloat16

NCORES = 8
D = 4096
DQ = 4096
DKV = 1024
HD = 128
NKV = 8
GQ = 4
FF = 11008
FSH = FF // NCORES
TOK = 1024
NB = 4
S = 2048
R = 64
SEL = 128
QBLK = 16
BIGM = 30000.0
SCALE = float(128.0 ** -0.5)

SLOT_A = [0, 3, 4, 7, 8, 11, 12, 15]
SLOT_B = [1, 2, 5, 6, 9, 10, 13, 14]
EXT = [2, 4, 6, 8, 10, 12, 14, 16]


def _row_perm():
    rows = []
    for b in range(NB):
        for slots in (SLOT_A, SLOT_B):
            for t in slots:
                base = 2048 * b + 128 * t
                rows.extend(range(base, base + 128))
    return np.asarray(rows, dtype=np.int64)



ENGINES = ("sync", "gpsimd", "vector", "scalar", "tensor")


class Prog:

    def __init__(self):
        self.ops = {e: [] for e in ENGINES}
        self.cnt = {}

    def bump(self, sem, by):
        self.cnt[sem] = self.cnt.get(sem, 0) + by
        return self.cnt[sem]

    def val(self, sem):
        return self.cnt.get(sem, 0)

    def dma(self, eng, out, in_, sem, **kw):
        tot = self.bump(sem, 16)

        def fn(e, _o=out, _i=in_, _s=sem, _k=kw):
            e.dma_start(out=_o, in_=_i, **_k).then_inc(self.sems[_s], 16)
        self.ops[eng].append(fn)
        return tot

    def wait(self, eng, sem, val):
        def fn(e, _s=sem, _v=val):
            e.wait_ge(self.sems[_s], _v)
        self.ops[eng].append(fn)

    CHAIN = {"vector": "vc", "scalar": "sc", "gpsimd": "gpc"}

    def op(self, eng, fn, sem=None):
        ch = self.CHAIN.get(eng)
        if ch is not None:
            self.wait(eng, ch, self.val(ch))
            tot = self.bump(ch, 1)

            def g(e, _f=fn, _c=ch):
                _f(e).then_inc(self.sems[_c], 1)
            self.ops[eng].append(g)
            return tot
        if sem is None:
            self.ops[eng].append(lambda e, _f=fn: _f(e))
            return None
        tot = self.bump(sem, 1)

        def g(e, _f=fn, _s=sem):
            _f(e).then_inc(self.sems[_s], 1)
        self.ops[eng].append(g)
        return tot

    def phase_barrier(self, engines=ENGINES):
        targets = [(n, v) for n, v in sorted(self.cnt.items()) if v]
        for eng in engines:
            for n, v in targets:
                self.wait(eng, n, v)

    def barrier(self, producer_eng, sem, waiters):
        tot = self.op(producer_eng, lambda e: e.engine_nop()
                      if producer_eng in ("vector", "scalar") else e.nop(), sem)
        for w in waiters:
            self.wait(w, sem, tot)
        return tot

    def replay(self, block, sems):
        self.sems = sems

        def mk(eng):
            def run(e):
                for fn in self.ops[eng]:
                    fn(e)
            return run
        block.sync(mk("sync"))
        block.gpsimd(mk("gpsimd"))
        block.vector(mk("vector"))
        block.scalar(mk("scalar"))
        block.tensor(mk("tensor"))


class Sbuf:
    BASE = 16544
    LIMIT = 229376

    def __init__(self, nc):
        self.nc = nc
        self.off = self.BASE

    def alloc(self, name, shape, dtype):
        nbytes = int(np.prod(shape[1:])) * mybir.dt.size(dtype)
        off = (self.off + 31) // 32 * 32
        assert off + nbytes <= self.LIMIT, (
            f"SBUF overflow: {name} at {off}+{nbytes} > {self.LIMIT}")
        t = self.nc.alloc_sbuf_tensor_at(name, list(shape), dtype, offset=off)
        self.off = off + nbytes
        return t


def _nop(eng_name):
    def fn(e):
        return e.engine_nop() if hasattr(e, "engine_nop") else e.nop()
    return fn



def emit_qdq(p, scr, v_ap, out_ap, P, F, SC="sc", scale_ap=None,
             use_gpsimd=False):
    A, Bs, Cs, Es, Ii, M0, M1, M2, SG = (
        scr["A"], scr["B"], scr["C"], scr["E"], scr["I"],
        scr["M0"], scr["M1"], scr["M2"], scr["SG"])
    nb = F // QBLK
    v3 = v_ap.rearrange("p (b c) -> p b c", c=QBLK)

    p.wait("scalar", "vc", p.val("vc"))
    p.op("scalar", lambda e: e.activation(SG[:P, :F], v_ap, AF.Sign))
    sgv = p.op("scalar", lambda e: e.activation(Es[:P, :F], v_ap, AF.Abs), SC)

    p.op("vector", lambda e: e.tensor_reduce(
        M0[:P, :nb].unsqueeze(2), v3, AX.X, ALU.max, apply_absolute_value=True))
    p.op("vector", lambda e: e.tensor_scalar(
        M0[:P, :nb], M0[:P, :nb], 1e-30, None, ALU.max))
    p.op("vector", lambda e: e.reciprocal(M1[:P, :nb], M0[:P, :nb]))
    p.op("vector", lambda e: e.tensor_scalar(
        M1[:P, :nb], M1[:P, :nb], 6.0, None, ALU.mult))
    p.op("vector", lambda e: e.tensor_scalar(
        M2[:P, :nb], M0[:P, :nb], 1.0 / 6.0, None, ALU.mult))
    m1b = M1[:P, :nb].unsqueeze(2).broadcast_to([P, nb, QBLK])
    m2b = M2[:P, :nb].unsqueeze(2).broadcast_to([P, nb, QBLK])

    p.wait("vector", SC, sgv)
    tv = p.op("vector", lambda e: e.tensor_tensor(
        Bs[:P, :F].rearrange("p (b c) -> p b c", c=QBLK),
        Es[:P, :F].rearrange("p (b c) -> p b c", c=QBLK), m1b, ALU.mult))

    geng = "gpsimd" if use_gpsimd else "vector"
    gch = "gpc" if use_gpsimd else "vc"
    If = Ii[:P, :F].bitcast(F32)
    p.op("vector", lambda e: e.tensor_scalar(
        Cs[:P, :F], Bs[:P, :F], 0.25, None, ALU.is_ge))
    for b in (0.75, 1.25, 1.75):
        p.op("vector", lambda e, _b=b: e.scalar_tensor_tensor(
            Cs[:P, :F], Bs[:P, :F], _b, Cs[:P, :F], ALU.is_ge, ALU.add))
    if use_gpsimd:
        p.wait(geng, "vc", tv)
    p.op(geng, lambda e: e.tensor_scalar(
        Es[:P, :F], Bs[:P, :F], 2.5, None, ALU.is_ge))
    p.op(geng, lambda e: e.scalar_tensor_tensor(
        Es[:P, :F], Bs[:P, :F], 3.5, Es[:P, :F], ALU.is_ge, ALU.add))
    gv = p.op(geng, lambda e: e.tensor_scalar(
        If, Bs[:P, :F], 5.0, 2.0, ALU.is_ge, ALU.mult))
    if use_gpsimd:
        p.wait("vector", gch, gv)
    p.op("vector", lambda e: e.scalar_tensor_tensor(
        Cs[:P, :F], Cs[:P, :F], 0.5, Es[:P, :F], ALU.mult, ALU.add))
    p.op("vector", lambda e: e.tensor_tensor(
        Cs[:P, :F], Cs[:P, :F], If, ALU.add))
    p.op("vector", lambda e: e.tensor_tensor(
        Bs[:P, :F].rearrange("p (b c) -> p b c", c=QBLK),
        SG[:P, :F].rearrange("p (b c) -> p b c", c=QBLK), m2b, ALU.mult))
    if scale_ap is not None:
        p.op("vector", lambda e: e.scalar_tensor_tensor(
            Cs[:P, :F], Cs[:P, :F], scale_ap, Bs[:P, :F], ALU.mult, ALU.mult))
        vcv = p.op("vector", lambda e: e.tensor_copy(out_ap, Cs[:P, :F]))
    else:
        vcv = p.op("vector", lambda e: e.tensor_tensor(
            out_ap, Cs[:P, :F], Bs[:P, :F], ALU.mult))
    return vcv



def build_launch1():
    nc = bass.Bass("TRN2", target_bir_lowering=False)
    sb = Sbuf(nc)
    p = Prog()

    xp = nc.dram_tensor("xp", [TOK, D], F32, kind="ExternalInput").ap()
    wq = nc.dram_tensor("wq", [D, DQ], F32, kind="ExternalInput").ap()
    wk = nc.dram_tensor("wk", [D, DKV], F32, kind="ExternalInput").ap()
    wv = nc.dram_tensor("wv", [D, DKV], F32, kind="ExternalInput").ap()
    la = nc.dram_tensor("la", [D, R], F32, kind="ExternalInput").ap()
    wgs = nc.dram_tensor("wgs", [FSH, D], F32, kind="ExternalInput").ap()
    wus = nc.dram_tensor("wus", [FSH, D], F32, kind="ExternalInput").ap()
    wds = nc.dram_tensor("wds", [D, FSH], F32, kind="ExternalInput").ap()
    wdm = nc.dram_tensor("wdm", [128, FSH], U8, kind="ExternalInput").ap()
    osc = nc.dram_tensor("osc", [128, D // 128], F32, kind="ExternalInput").ap()
    idn = nc.dram_tensor("idn", [128, 128], BF16, kind="ExternalInput").ap()

    xq_o = nc.dram_tensor("xq_o", [TOK, D], BF16, kind="ExternalOutput").ap()
    qt_o = nc.dram_tensor("qt_o", [DQ, TOK], BF16, kind="ExternalOutput").ap()
    kt_o = nc.dram_tensor("kt_o", [DKV, TOK], BF16, kind="ExternalOutput").ap()
    v_o = nc.dram_tensor("v_o", [TOK, DKV], BF16, kind="ExternalOutput").ap()
    o1t_o = nc.dram_tensor("o1t_o", [R, TOK], BF16, kind="ExternalOutput").ap()
    wg_o = nc.dram_tensor("wg_o", [FSH, D], BF16, kind="ExternalOutput").ap()
    wu_o = nc.dram_tensor("wu_o", [FSH, D], BF16, kind="ExternalOutput").ap()
    wd_o = nc.dram_tensor("wd_o", [D, FSH], BF16, kind="ExternalOutput").ap()

    xn_d = nc.dram_tensor("xn_d", [TOK, D], BF16).ap()

    HW = D // 2
    Bt = sb.alloc("B", [128, HW], F32)
    Ct = sb.alloc("C", [128, HW], F32)
    Et = sb.alloc("E", [128, HW], F32)
    It = sb.alloc("I", [128, HW], I32)
    SG = sb.alloc("SG", [128, HW], F32)
    A = sb.alloc("A", [128, D], F32)
    XA = [sb.alloc(f"XA{i}", [128, D], F32) for i in range(2)]
    XNB = sb.alloc("XNB", [128, D], BF16)
    XQB = sb.alloc("XQB", [128, D], BF16)
    XNT = sb.alloc("XNT", [128, 32 * TOK], BF16)
    WB = [sb.alloc(f"WB{i}", [128, 32 * 128], BF16) for i in range(2)]
    QSTG = [sb.alloc(f"QSTG{i}", [128, TOK], BF16) for i in range(2)]
    VSTG = [sb.alloc(f"VSTG{i}", [128, TOK], BF16) for i in range(2)]
    LAB = sb.alloc("LAB", [128, 32 * R], BF16)
    O1S = sb.alloc("O1S", [64, TOK], BF16)
    M0 = sb.alloc("M0", [128, D // QBLK], F32)
    M1 = sb.alloc("M1", [128, D // QBLK], F32)
    M2 = sb.alloc("M2", [128, D // QBLK], F32)
    ST1 = sb.alloc("ST1", [128, 1], F32)
    ST2 = sb.alloc("ST2", [128, 1], F32)
    TMP1 = sb.alloc("TMP1", [128, 1], F32)
    TMP2 = sb.alloc("TMP2", [128, 1], F32)
    RSTD = sb.alloc("RSTD", [128, 1], F32)
    NBIAS = sb.alloc("NBIAS", [128, 1], F32)
    WMSK = sb.alloc("WMSK", [128, FSH], U8)
    OSC = sb.alloc("OSC", [128, D // 128], F32)
    IDN = sb.alloc("IDN", [128, 128], BF16)
    RAWB = Bt

    scr = {"A": A, "B": Bt, "C": Ct, "E": Et, "I": It,
           "M0": M0, "M1": M1, "M2": M2, "SG": SG}

    ps = [nc.alloc_psum_tensor(f"ps{i}", [128, 512], F32) for i in range(4)]
    pst_t = nc.alloc_psum_tensor("pst_t", [128, 128], BF16)

    LD, STs, VC, SC, MM, GP = "ld", "st", "vc", "sc", "mm", "gp"

    p.dma("sync", WMSK[:], wdm, LD)
    p.dma("sync", OSC[:], osc, LD)
    p.dma("sync", IDN[:], idn, LD)
    const_ld = p.val(LD)

    tiles = []
    for i in range(TOK // 128):
        tiles.append(("x", i))
    for tag in ("wg", "wu"):
        for r0 in range(0, FSH, 128):
            tiles.append((tag, r0))
    for r0 in range(0, D, 128):
        tiles.append(("wd", r0))

    ld_sem = ["lda", "ldb"]
    st_sem = ["sta", "stb"]
    ld_val = {}
    st_val = {}
    qdq_val = {}

    def xnt(t):
        return XNT[:, TOK * t:TOK * (t + 1)]

    blocks = []
    for o in range(32):
        blocks.append(("q", o))
    for o in range(8):
        blocks.append(("k", o))
    for o in range(8):
        blocks.append(("v", o))
    wsrc_map = {"q": wq, "k": wk, "v": wv}
    dst_map = {"q": qt_o, "k": kt_o, "v": v_o}
    gp_sem = ["gpa", "gpb"]
    pt_sem = ["pta", "ptb"]
    gld = {}
    mm_done = {}
    ev_done = {}
    xnt_ld = [None]

    def load_block(bi):
        kindb, o = blocks[bi]
        bpar = bi % 2
        if bi >= 2:
            p.wait("gpsimd", MM, mm_done[bi - 2])
        gld[bi] = p.dma("gpsimd",
                        WB[bpar][:].rearrange("p (t c) -> p t c", c=128),
                        wsrc_map[kindb][:, 128 * o:128 * (o + 1)].rearrange(
                            "(t p) c -> p t c", p=128), gp_sem[bpar])

    def emit_xnt():
        p.wait("sync", "sta", p.val("sta"))
        p.wait("sync", "stb", p.val("stb"))
        for t in range(32):
            p.dma("sync", XNT[:, TOK * t:TOK * (t + 1)],
                  xn_d[:, 128 * t:128 * (t + 1)], LD, transpose=True)
        xnt_ld[0] = p.val(LD)

    def emit_block(bi):
        kindb, o = blocks[bi]
        bpar = bi % 2
        p.wait("tensor", gp_sem[bpar], gld[bi])
        if bi == 0:
            p.wait("tensor", LD, xnt_ld[0])
        if bi >= 2:
            p.wait("tensor", "vc", ev_done[bi - 2])
        for nchunk in range(2):
            pb = ps[2 * bpar + nchunk]
            for dt in range(32):
                mmv = p.op("tensor",
                           lambda e, _b=bpar, _dt=dt, _n=nchunk, _p=pb:
                           e.matmul(
                               _p[:, 0:512],
                               WB[_b][:, 128 * _dt:128 * (_dt + 1)],
                               xnt(_dt)[:, 512 * _n:512 * (_n + 1)],
                               start=(_dt == 0), stop=(_dt == 31)),
                           MM if dt == 31 else None)
        mm_done[bi] = mmv
        if bi + 2 < len(blocks):
            load_block(bi + 2)
        p.wait("vector", MM, mmv)
        if bi >= 2:
            p.wait("vector", pt_sem[bpar], st_val.get(("p2", bi - 2), 0))
        for nchunk in range(2):
            pb = ps[2 * bpar + nchunk]
            vcv = p.op("vector", lambda e, _p=pb, _n=nchunk, _q=bpar:
                       e.tensor_copy(QSTG[_q][:, 512 * _n:512 * (_n + 1)],
                                     _p[:, 0:512]))
        ev_done[bi] = vcv
        if kindb != "v":
            p.wait("sync", "vc", vcv)
            st_val[("p2", bi)] = p.dma(
                "sync", dst_map[kindb][128 * o:128 * (o + 1), :],
                QSTG[bpar][:], pt_sem[bpar])
        else:
            p.wait("tensor", "vc", ev_done[bi])
            for tt in range(8):
                mmv2 = p.op("tensor", lambda e, _tt=tt, _q=bpar: e.transpose(
                    pst_t[:], QSTG[_q][:, 128 * _tt:128 * (_tt + 1)], IDN[:]),
                    MM)
                p.wait("vector", MM, mmv2)
                vcv = p.op("vector", lambda e, _tt=tt, _q=bpar: e.tensor_copy(
                    VSTG[_q][:, 128 * _tt:128 * (_tt + 1)], pst_t[:]))
                p.wait("tensor", "vc", vcv)
            p.wait("sync", "vc", vcv)
            for tt in range(8):
                st_val[("p2", bi)] = p.dma(
                    "sync", dst_map[kindb][128 * tt:128 * (tt + 1),
                                           128 * o:128 * (o + 1)],
                    VSTG[bpar][:, 128 * tt:128 * (tt + 1)], pt_sem[bpar])
            mm_done[bi] = mmv2

    lab_ld = p.dma("gpsimd", LAB[:].rearrange("p (t r) -> p t r", r=R),
                   la.rearrange("(t p) r -> p t r", p=128), "lab")
    load_block(0)
    load_block(1)

    for ti, (kind, idx) in enumerate(tiles):
        par = ti % 2
        xa = XA[par]
        xnb = XNB[par]
        if ti >= 2:
            p.wait("sync", "vc", qdq_val[ti - 2])
        if kind == "x":
            src_ap = xp[128 * idx:128 * (idx + 1), :]
            rows, Fw = 128, D
        elif kind == "wg":
            rows = min(128, FSH - idx)
            src_ap = wgs[idx:idx + rows, :]
            Fw = D
        elif kind == "wu":
            rows = min(128, FSH - idx)
            src_ap = wus[idx:idx + rows, :]
            Fw = D
        else:
            rows, Fw = 128, FSH
            src_ap = wds[idx:idx + rows, :]
        ld_val[ti] = p.dma("sync", xa[:rows, :Fw], src_ap, ld_sem[par])
        p.wait("vector", ld_sem[par], ld_val[ti])
        p.wait("scalar", ld_sem[par], ld_val[ti])
        if ti == 0:
            p.wait("vector", LD, const_ld)
            p.wait("scalar", LD, const_ld)

        if kind == "x":
            if ti >= 1:
                p.wait("scalar", "vc", qdq_val[ti - 1])
            p.op("vector", lambda e, _x=xa: e.reduce_sum(ST1[:], _x[:], AX.X))
            p.op("scalar", lambda e, _x=xa: e.activation(
                Ct[:], _x[:, 0:HW], AF.Square))
            scv = p.op("scalar", lambda e, _x=xa: e.activation(
                Et[:], _x[:, HW:D], AF.Square), SC)
            p.wait("vector", SC, scv)
            p.op("vector", lambda e: e.reduce_sum(ST2[:], Ct[:], AX.X))
            p.op("vector", lambda e: e.reduce_sum(TMP2[:], Et[:], AX.X))
            p.op("vector", lambda e: e.tensor_tensor(
                ST2[:], ST2[:], TMP2[:], ALU.add))
            p.op("vector", lambda e: e.tensor_scalar(
                ST1[:], ST1[:], 1.0 / D, None, ALU.mult))
            p.op("vector", lambda e: e.tensor_scalar(
                ST2[:], ST2[:], 1.0 / D, None, ALU.mult))
            p.op("vector", lambda e: e.tensor_tensor(
                TMP1[:], ST1[:], ST1[:], ALU.mult))
            p.op("vector", lambda e: e.tensor_tensor(
                ST2[:], ST2[:], TMP1[:], ALU.subtract))
            vcv = p.op("vector", lambda e: e.tensor_scalar(
                ST2[:], ST2[:], 1e-5, None, ALU.add))
            p.wait("scalar", "vc", vcv)
            p.op("scalar", lambda e: e.activation(RSTD[:], ST2[:], AF.Ln))
            scv = p.op("scalar", lambda e: e.activation(
                RSTD[:], RSTD[:], AF.Exp, scale=-0.5), SC)
            p.wait("vector", SC, scv)
            p.op("vector", lambda e: e.tensor_tensor(
                NBIAS[:], ST1[:], RSTD[:], ALU.mult))
            p.op("vector", lambda e: e.tensor_scalar(
                NBIAS[:], NBIAS[:], -1.0, None, ALU.mult))
            p.op("vector", lambda e, _x=xa: e.tensor_scalar(
                A[:], _x[:], RSTD[:], NBIAS[:], ALU.mult, ALU.add))
            if ti >= 1:
                p.wait("vector", st_sem[(ti - 1) % 2], st_val[ti - 1])
            vcv = p.op("vector", lambda e: e.tensor_copy(XNB[:], A[:]))
            p.wait("scalar", "vc", vcv)
            emit_qdq(p, scr, A[:, 0:HW], XQB[:, 0:HW], 128, HW, SC)
            emit_qdq(p, scr, A[:, HW:D], XQB[:, HW:D], 128, HW, SC)
            vcv = p.op("vector", lambda e: e.tensor_copy(
                XQB[:, D - SEL:D], XNB[:, D - SEL:D]))
            qdq_val[ti] = vcv
            p.wait("sync", "vc", vcv)
            p.dma("sync", xn_d[128 * idx:128 * (idx + 1), :], XNB[:],
                  st_sem[par])
            st_val[ti] = p.dma("sync", xq_o[128 * idx:128 * (idx + 1), :],
                               XQB[:], st_sem[par])
        else:
            scaled = kind == "wd"
            sc_ap = OSC[:rows, idx // 128:idx // 128 + 1] if scaled else None
            if ti >= 1:
                p.wait("vector", st_sem[(ti - 1) % 2], st_val[ti - 1])
                p.wait("scalar", "vc", qdq_val[ti - 1])
            h2 = Fw // 2
            emit_qdq(p, scr, xa[:rows, 0:h2], XNB[:rows, 0:h2], rows, h2,
                     SC, scale_ap=sc_ap)
            vcv = emit_qdq(p, scr, xa[:rows, h2:Fw], XNB[:rows, h2:Fw],
                           rows, h2, SC, scale_ap=sc_ap)
            if kind in ("wg", "wu"):
                vcv = p.op("vector", lambda e, _r=rows, _x=xa:
                           e.tensor_copy(XNB[:_r, D - SEL:D],
                                         _x[:_r, D - SEL:D]))
            else:
                p.op("vector", lambda e, _r=rows, _x=xa, _s=sc_ap:
                     e.tensor_scalar(RAWB[:_r, :FSH], _x[:_r, :FSH], _s,
                                     None, ALU.mult))
                vcv = p.op("vector", lambda e, _r=rows:
                           e.copy_predicated(XNB[:_r, :FSH], WMSK[:_r, :FSH],
                                             RAWB[:_r, :FSH]))
            qdq_val[ti] = vcv
            p.wait("sync", "vc", vcv)
            dst = {"wg": wg_o, "wu": wu_o, "wd": wd_o}[kind]
            st_val[ti] = p.dma("sync", dst[idx:idx + rows, :],
                               XNB[:rows, :Fw], st_sem[par])

        if ti == 8:
            emit_xnt()
        bi = ti - 9
        if 0 <= bi < len(blocks):
            emit_block(bi)

    p.wait("tensor", "lab", lab_ld)
    p.wait("tensor", "vc", ev_done[len(blocks) - 1])
    for nchunk in range(2):
        pb = ps[2 + nchunk]
        for dt in range(32):
            mmv = p.op("tensor", lambda e, _dt=dt, _n=nchunk, _p=pb: e.matmul(
                _p[0:64, 0:512], LAB[:, R * _dt:R * (_dt + 1)],
                xnt(_dt)[:, 512 * _n:512 * (_n + 1)],
                start=(_dt == 0), stop=(_dt == 31)),
                MM if dt == 31 else None)
        p.wait("vector", MM, mmv)
        vcv = p.op("vector", lambda e, _p=pb, _n=nchunk: e.tensor_copy(
            O1S[0:64, 512 * _n:512 * (_n + 1)], _p[0:64, 0:512]))
    p.wait("sync", "vc", vcv)
    p.dma("sync", o1t_o, O1S[0:64, :], STs)

    p.phase_barrier(("sync",))

    sems = {}
    stack = contextlib.ExitStack()
    for name in (LD, STs, VC, SC, MM, GP, "gpc", "lda", "ldb",
                 "sta", "stb", "gpa", "gpb", "pta", "ptb", "lab"):
        sems[name] = stack.enter_context(nc.semaphore(name))
    with nc.Block() as block:
        p.replay(block, sems)
    stack.close()
    return nc



def _launch1_inputs(inputs):
    X = np.asarray(inputs["L0_attn_hidden_states"]).reshape(NB * S, D)
    up = np.asarray(inputs["L0_ffn_up_reorder_index"]).astype(np.int64)
    dn = np.asarray(inputs["L0_ffn_down_reorder_index"]).astype(np.int64)
    PR = _row_perm()

    Xp = np.ascontiguousarray(X[PR][:, up])
    WQ = np.ascontiguousarray(np.asarray(inputs["L0_attn_w_q"])[up, :])
    WK = np.ascontiguousarray(np.asarray(inputs["L0_attn_w_k"])[up, :])
    WV = np.ascontiguousarray(np.asarray(inputs["L0_attn_w_v"])[up, :])
    LA = np.ascontiguousarray(np.asarray(inputs["L0_ffn_lr_A"])[up, :])
    WGp = np.asarray(inputs["L0_ffn_w_gate"])[dn][:, up]
    WUp = np.asarray(inputs["L0_ffn_w_up"])[dn][:, up]
    WDp = np.asarray(inputs["L0_ffn_w_down"])[:, dn]
    oscale = np.asarray(inputs["L0_ffn_mlp_output_scale"]).astype(np.float32)
    osc = np.ascontiguousarray(oscale.reshape(D // 128, 128).T)
    idn = np.eye(128, dtype=BF)

    in_maps = []
    for c in range(NCORES):
        f0 = FSH * c
        keep = (np.arange(f0, f0 + FSH) >= FF - SEL).astype(np.uint8)
        wdmask = np.ascontiguousarray(np.broadcast_to(keep, (128, FSH)))
        in_maps.append({
            "xp": np.ascontiguousarray(Xp[TOK * c:TOK * (c + 1)]),
            "wq": WQ, "wk": WK, "wv": WV, "la": LA,
            "wgs": np.ascontiguousarray(WGp[f0:f0 + FSH]),
            "wus": np.ascontiguousarray(WUp[f0:f0 + FSH]),
            "wds": np.ascontiguousarray(WDp[:, f0:f0 + FSH]),
            "wdm": wdmask, "osc": osc, "idn": idn,
        })
    return in_maps, PR



FCH = 512
NFCH = 22
CW = [512] * 21 + [256]
FT = FF // 128
FHALF = FT // 2


def build_launch2():
    nc = bass.Bass("TRN2", target_bir_lowering=False)
    sb = Sbuf(nc)
    p = Prog()

    xres = nc.dram_tensor("xres", [TOK, D], F32, kind="ExternalInput").ap()
    xqt = nc.dram_tensor("xqt", [128, 32 * TOK], BF16,
                         kind="ExternalInput").ap()
    qt = nc.dram_tensor("qt", [DQ, TOK], BF16, kind="ExternalInput").ap()
    ktb = nc.dram_tensor("ktb", [DKV, S], BF16, kind="ExternalInput").ap()
    vaug = nc.dram_tensor("vaug", [S, NKV * 129], BF16, kind="ExternalInput").ap()
    o1t = nc.dram_tensor("o1t", [R, TOK], BF16, kind="ExternalInput").ap()
    lrb = nc.dram_tensor("lrb", [R, D], F32, kind="ExternalInput").ap()
    wo = nc.dram_tensor("wo", [8 * 128, 32 * 512], F32,
                        kind="ExternalInput").ap()
    wgt = nc.dram_tensor("wgt", [NFCH * 128, 32 * FCH], BF16,
                         kind="ExternalInput").ap()
    wut = nc.dram_tensor("wut", [NFCH * 128, 32 * FCH], BF16,
                         kind="ExternalInput").ap()
    wdt = nc.dram_tensor("wdt", [16 * 128, FHALF * 512], BF16,
                         kind="ExternalInput").ap()
    mskr = nc.dram_tensor("mskr", [8, 2, 128, 128], BF16, kind="ExternalInput").ap()
    ucst = nc.dram_tensor("ucst", [128, 128], BF16, kind="ExternalInput").ap()
    one1 = nc.dram_tensor("one1", [1, 128], BF16, kind="ExternalInput").ap()
    onec = nc.dram_tensor("onec", [128, 1], BF16, kind="ExternalInput").ap()
    idn = nc.dram_tensor("idn", [128, 128], BF16, kind="ExternalInput").ap()

    out = nc.dram_tensor("out", [TOK, D], F32, kind="ExternalOutput").ap()

    attnt_d = nc.dram_tensor("attnt_d", [128, 32 * TOK], BF16).ap()
    hqt_d = nc.dram_tensor("hqt_d", [FF, TOK], BF16).ap()
    down_d = nc.dram_tensor("down_d", [TOK, D], F32).ap()

    OV = Sbuf.BASE

    def ov(name, shape, dtype, off):
        return nc.alloc_sbuf_tensor_at(name, list(shape), dtype, offset=OV + off)

    KTS = ov("KTS", [128, NKV * S], BF16, 0)
    VAS = ov("VAS", [128, 16 * 1032], BF16, 32768)
    QTS = ov("QTS", [128, 32 * TOK], BF16, 65792)
    KSQ = ov("KSQ", [128, S], BF16, 131328)
    PT = [ov(f"PT{i}", [128, 512], BF16, 135424 + 1024 * i) for i in range(2)]
    ATL = ov("ATL", [128, 128], BF16, 137472)
    ATLT = ov("ATLT", [128, 128], BF16, 137728)
    MSK = ov("MSK", [128, 8 * 2 * 128], BF16, 137984)
    XQS = ov("XQS", [128, 32 * TOK], BF16, 0)
    WGC = [ov(f"WGC{i}", [128, 32 * FCH], BF16, 65536 + 32768 * i)
           for i in range(2)]
    WUC = ov("WUC", [128, 32 * FCH], BF16, 131072)
    QS = {}
    boff = 163840
    for nm, dt_ in (("A", F32), ("B", F32), ("C", F32), ("E", F32),
                    ("I", I32), ("SG", F32)):
        QS[nm] = ov("q" + nm, [128, FCH], dt_, boff)
        boff += FCH * 4
    H_ = ov("H_", [128, FCH], F32, boff); boff += FCH * 4
    SILB = ov("SILB", [128, 8 * FCH], F32, boff); boff += 8 * FCH * 4
    HQB = [None] * 4
    for _ti in range(4):
        HQB[_ti] = ov(f"HQ{_ti}", [128, FCH], BF16, boff); boff += FCH * 2
    HQTW2 = ov("HQW2", [128, 4 * TOK], BF16, boff)
    boff += 4 * TOK * 2
    WDC = [ov("WDCa", [128, FHALF * 512], BF16, 0),
           ov("WDCb", [128, FHALF * 512], BF16, 69888)]
    HQF = [ov(f"HQF{i}", [128, TOK], BF16, 45056 + 2048 * i)
           for i in range(4)]
    DSTG = [ov(f"DSTG{i}", [128, 512], F32, 53248 + 2048 * i)
            for i in range(2)]
    ATT2F = ov("ATT2F", [128, 32 * TOK], BF16, 113920)
    WOC = [ov(f"WOC{i}", [128, 32 * 512], BF16, 32768 * i)
           for i in range(2)]
    LRB = ov("LRB", [64, D], BF16, 69888)
    DT2 = [ov(f"DT{i}", [128, 512], F32, 78080 + 2048 * i) for i in range(2)]
    XT2 = [ov(f"XT{i}", [128, 512], F32, 82176 + 2048 * i) for i in range(2)]
    OS2 = [ov(f"OS{i}", [128, 512], F32, 86272 + 2048 * i) for i in range(2)]

    sb.off = Sbuf.BASE + 206848
    UC = sb.alloc("UC", [128, 128], BF16)
    IDN = sb.alloc("IDN", [128, 128], BF16)
    O1S = sb.alloc("O1S", [64, TOK], BF16)
    ONE1 = sb.alloc("ONE1", [1, 128], BF16)
    ONEC = sb.alloc("ONEC", [128, 1], BF16)
    KN2 = sb.alloc("KN2", [1, NKV], F32)
    QN2 = sb.alloc("QN2", [1, 32], F32)
    TR1 = sb.alloc("TR1", [1, 1], F32)
    TR2 = sb.alloc("TR2", [1, 1], F32)
    BNDB = sb.alloc("BNDB", [1, 32], BF16)
    BIASC = sb.alloc("BIASC", [128, 32], F32)
    RZ = sb.alloc("RZ", [128, 1], F32)
    M0 = sb.alloc("M0", [128, FCH // QBLK], F32)
    M1 = sb.alloc("M1", [128, FCH // QBLK], F32)
    M2 = sb.alloc("M2", [128, FCH // QBLK], F32)
    QS["M0"], QS["M1"], QS["M2"] = M0, M1, M2

    ps = [nc.alloc_psum_tensor(f"ps{i}", [128, 512], F32) for i in range(8)]
    psT = ps[6][:, 0:64].bitcast(BF16)

    LD, STs, MM, GP = "ld", "st", "mm", "gp"

    p.dma("sync", KTS[:].rearrange("p (h k) -> p h k", k=S),
          ktb.rearrange("(h p) k -> p h k", p=128), LD)
    p.dma("sync", VAS[:].rearrange("p (t c) -> p t c", c=1032),
          vaug.rearrange("(t p) c -> p t c", p=128), LD)
    p.dma("sync", QTS[:].rearrange("p (h q) -> p h q", q=TOK),
          qt.rearrange("(h p) q -> p h q", p=128), LD)
    p.dma("sync", MSK[:].rearrange("p (s j q) -> p s j q", j=2, q=128),
          mskr.rearrange("s j p q -> p s j q"), LD)
    p.dma("sync", UC[:], ucst, LD)
    p.dma("sync", IDN[:], idn, LD)
    p.dma("sync", O1S[0:64, :], o1t, LD)
    p.dma("sync", ONE1[0:1, :], one1, LD)
    ldv = p.dma("sync", ONEC[:], onec, LD)
    p.wait("scalar", LD, ldv)
    p.wait("tensor", LD, ldv)
    p.wait("vector", LD, ldv)

    def norms_for(block_ap, width, out_row, out_col):
        p.wait("scalar", MM, p.val(MM))
        scv = p.op("scalar", lambda e, _b=block_ap, _w=width: e.activation(
            KSQ[:, 0:_w], _b, AF.Square))
        p.wait("tensor", "sc", scv)
        for cidx in range(width // 512):
            mmv = p.op("tensor", lambda e, _c=cidx: e.matmul(
                ps[7][0:1, 0:512], ONEC[:],
                KSQ[:, 512 * _c:512 * (_c + 1)], start=True, stop=True), MM)
            p.wait("vector", MM, mmv)
            if cidx == 0:
                vcv = p.op("vector", lambda e: e.tensor_reduce(
                    TR1[0:1, :], ps[7][0:1, 0:512], AX.X, ALU.max))
            else:
                p.op("vector", lambda e: e.tensor_reduce(
                    TR2[0:1, :], ps[7][0:1, 0:512], AX.X, ALU.max))
                vcv = p.op("vector", lambda e: e.tensor_tensor(
                    TR1[0:1, :], TR1[0:1, :], TR2[0:1, :], ALU.max))
            p.wait("tensor", "vc", vcv)
        p.op("vector", lambda e, _o=out_col, _r=out_row: e.tensor_copy(
            _r[0:1, _o:_o + 1], TR1[0:1, :]))

    for h in range(NKV):
        norms_for(KTS[:, S * h:S * (h + 1)], S, KN2, h)
    for H in range(32):
        norms_for(QTS[:, TOK * H:TOK * (H + 1)], TOK, QN2, H)

    for kvh in range(NKV):
        p.op("vector", lambda e, _h=kvh: e.tensor_reduce(
            TR1[0:1, :], QN2[0:1, 4 * _h:4 * (_h + 1)], AX.X, ALU.max))
        p.op("vector", lambda e, _h=kvh: e.tensor_tensor(
            TR1[0:1, :], TR1[0:1, :], KN2[0:1, _h:_h + 1], ALU.add))
        vcv = p.op("vector", lambda e, _h=kvh: e.tensor_scalar(
            BNDB[0:1, _h:_h + 1], TR1[0:1, :], -SCALE / 2.0, None, ALU.mult))
        p.wait("tensor", "vc", vcv)
        mmv = p.op("tensor", lambda e, _h=kvh: e.matmul(
            ps[7][:, 0:1], ONE1[0:1, :], BNDB[0:1, _h:_h + 1],
            start=True, stop=True), MM)
        p.wait("vector", MM, mmv)
        vcv = p.op("vector", lambda e, _h=kvh: e.tensor_copy(
            BIASC[:, _h:_h + 1], ps[7][:, 0:1]))
        p.wait("tensor", "vc", vcv)

    for kvh in range(NKV):
        for slot in range(8):
            E = EXT[slot]
            exp_done = [None] * E

            def sc_stage2(kt):
                pa = ps[kt % 2]
                if kt >= 2:
                    p.wait("tensor", "sc", exp_done[kt - 2])
                lm = kt >= E - 2
                last_mm = None
                for qh in range(4):
                    H = 4 * kvh + qh
                    is_last = (qh == 3) and not lm
                    last_mm = p.op("tensor",
                                   lambda e, _kvh=kvh, _H=H, _s=slot, _kt=kt,
                                   _pa=pa, _lm=lm, _qh=qh: e.matmul(
                                       _pa[:, 128 * _qh:128 * (_qh + 1)],
                                       KTS[:, S * _kvh + 128 * _kt:
                                           S * _kvh + 128 * (_kt + 1)],
                                       QTS[:, TOK * _H + 128 * _s:
                                           TOK * _H + 128 * (_s + 1)],
                                       start=(_qh == 0),
                                       stop=(_qh == 3 and not _lm)),
                                   MM if is_last else None)
                    if lm:
                        j = kt - (E - 2)
                        last_mm = p.op("tensor",
                                       lambda e, _pa=pa, _s=slot, _j=j,
                                       _qh=qh: e.matmul(
                                           _pa[:, 128 * _qh:128 * (_qh + 1)],
                                           UC[:],
                                           MSK[:, (2 * _s + _j) * 128:
                                               (2 * _s + _j + 1) * 128],
                                           start=False, stop=(_qh == 3)),
                                       MM if qh == 3 else None)
                return last_mm

            def exp_stage(kt, mmv):
                pa = ps[kt % 2]
                p.wait("scalar", MM, mmv)
                ex = p.op("scalar", lambda e, _pa=pa, _k=kt, _h=kvh: e.activation(
                    PT[_k % 2][:, 0:512], _pa[:, 0:512], AF.Exp,
                    scale=SCALE, bias=BIASC[:, _h:_h + 1]))
                exp_done[kt] = ex

            def av_stage(kt):
                pat0 = 2
                p.wait("tensor", "sc", exp_done[kt])
                lastv = None
                for qh in range(4):
                    lastv = p.op("tensor",
                                 lambda e, _k=kt, _kvh=kvh, _qh=qh, _E=E: e.matmul(
                                     ps[2 + _qh][:, 0:129], PT[_k % 2][:, 128 * _qh:128 * (_qh + 1)],
                                     VAS[:, 1032 * _k + 129 * _kvh:
                                         1032 * _k + 129 * (_kvh + 1)],
                                     start=(_k == 0), stop=(_k == _E - 1)),
                                 MM if (kt == E - 1 and qh == 3) else None)
                return lastv

            mm_prev = sc_stage2(0)
            exp_stage(0, mm_prev)
            for kt in range(1, E):
                mm_prev = sc_stage2(kt)
                av_stage(kt - 1)
                exp_stage(kt, mm_prev)
            av_last = av_stage(E - 1)

            p.wait("vector", MM, av_last)
            for qh in range(4):
                H = 4 * kvh + qh
                p.op("vector", lambda e, _q=qh: e.reciprocal(
                    RZ[:], ps[2 + _q][:, 128:129]))
                vcv = p.op("vector", lambda e, _q=qh: e.tensor_scalar(
                    ATL[:], ps[2 + _q][:, 0:128], RZ[:], None, ALU.mult))
                p.wait("tensor", "vc", vcv)
                mmv = p.op("tensor", lambda e: e.transpose(
                    psT[:, 0:128], ATL[:], IDN[:]), MM)
                p.wait("vector", MM, mmv)
                vcv = p.op("vector", lambda e: e.tensor_copy(
                    ATLT[:], psT[:, 0:128]))
                p.wait("tensor", "vc", vcv)
                p.wait("sync", "vc", vcv)
                stv = p.dma("sync",
                            attnt_d[:, TOK * H + 128 * slot:
                                    TOK * H + 128 * (slot + 1)],
                            ATLT[:], STs)
                p.wait("vector", STs, stv)

    p.phase_barrier()

    xq_ld = p.dma("sync", XQS[:], xqt, "xq")
    g_ld, u_ld = {}, {}
    g_mm, u_mm = {}, {}
    silu_done, hev, qdq_done = {}, {}, {}
    tdone, tcopy_done, st_done = {}, {}, {}
    h_sc_guard = {}

    def load_g(ch):
        buf = ch % 2
        if ch >= 2:
            p.wait("sync", MM, g_mm[(ch - 2, 7)])
        g_ld[ch] = p.dma("sync", WGC[buf][:],
                         wgt[128 * ch:128 * (ch + 1), :],
                         "wga" if buf == 0 else "wgb")

    def load_u(ch):
        if ch >= 1:
            p.wait("sync", MM, u_mm[(ch - 1, 7)])
        u_ld[ch] = p.dma("sync", WUC[:], wut[128 * ch:128 * (ch + 1), :],
                         "wua" if ch % 2 == 0 else "wub")

    def trans_block(ch, tt):
        w = CW[ch]
        nj = w // 128
        hq = HQB[tt % 4]
        p.wait("tensor", "vc", qdq_done[(ch, tt)])
        if tt == 0 and ch >= 1:
            p.wait("scalar", *st_done[ch - 1])
        for j in range(nj):
            mmv = p.op("tensor", lambda e, _j=j, _h=hq, _tt=tt, _nj=nj:
                       e.matmul(
                           ps[_tt][:, 64 * _j:64 * (_j + 1)].bitcast(BF16),
                           _h[:, 128 * _j:128 * (_j + 1)], IDN[:],
                           start=(_j == 0), stop=(_j == _nj - 1),
                           is_transpose=True),
                       MM if j == nj - 1 else None)
        tdone[(ch, tt)] = mmv
        p.wait("scalar", MM, mmv)
        scv = p.op("scalar", lambda e, _tt=tt, _nj=nj: e.activation(
            HQTW2[:].rearrange("p (j t q) -> p j t q", j=4, q=128)
            [:, 0:_nj, _tt, :],
            ps[_tt][:, 0:64 * _nj].bitcast(BF16).rearrange(
                "p (j q) -> p j q", q=128), AF.Copy))
        tcopy_done[(ch, tt)] = scv
        if tt == 7:
            p.wait("sync", "sc", scv)
            ssem = "sta" if ch % 2 == 0 else "stb"
            for j in range(nj):
                p.dma("sync",
                      hqt_d[FCH * ch + 128 * j:FCH * ch + 128 * (j + 1), :],
                      HQTW2[:, TOK * j:TOK * (j + 1)], ssem)
            st_done[ch] = (ssem, p.val(ssem))

    def g_block(ch, tt):
        buf = ch % 2
        w = CW[ch]
        if ch > 0 and tt >= 5:
            trans_block(ch - 1, tt)
        if tt == 0:
            p.wait("tensor", "wga" if buf == 0 else "wgb", g_ld[ch])
            if ch == 0:
                p.wait("tensor", "xq", xq_ld)
        if ch > 0:
            p.wait("tensor", "sc", tcopy_done[(ch - 1, tt)])
        for dt in range(32):
            mmv = p.op("tensor", lambda e, _dt=dt, _tt=tt, _b=buf, _w=w:
                       e.matmul(
                           ps[_tt][:, 0:_w],
                           XQS[:, TOK * _dt + 128 * _tt:
                               TOK * _dt + 128 * (_tt + 1)],
                           WGC[_b][:, FCH * _dt:FCH * _dt + _w],
                           start=(_dt == 0), stop=(_dt == 31)),
                       MM if dt == 31 else None)
        g_mm[(ch, tt)] = mmv
        p.wait("scalar", MM, mmv)
        if ch > 0:
            p.wait("scalar", "vc", hev[(ch - 1, tt)])
        silu_done[(ch, tt)] = p.op(
            "scalar", lambda e, _tt=tt, _w=w: e.activation(
                SILB[:, FCH * _tt:FCH * _tt + _w], ps[_tt][:, 0:_w],
                AF.Silu))

    def u_block(ch, tt):
        w = CW[ch]
        if tt >= 3:
            trans_block(ch, tt - 3)
        if tt == 0:
            p.wait("tensor", "wua" if ch % 2 == 0 else "wub", u_ld[ch])
        p.wait("tensor", "sc", silu_done[(ch, tt)])
        for dt in range(32):
            mmv = p.op("tensor", lambda e, _dt=dt, _tt=tt, _w=w: e.matmul(
                ps[_tt][:, 0:_w],
                XQS[:, TOK * _dt + 128 * _tt:TOK * _dt + 128 * (_tt + 1)],
                WUC[:, FCH * _dt:FCH * _dt + _w],
                start=(_dt == 0), stop=(_dt == 31)),
                MM if dt == 31 else None)
        u_mm[(ch, tt)] = mmv
        p.wait("vector", MM, mmv)
        prevk = (ch, tt - 4) if tt >= 4 else (ch - 1, tt + 4)
        if prevk in tdone:
            p.wait("vector", MM, tdone[prevk])
        p.wait("vector", "sc", p.val("sc"))
        vcv = p.op("vector", lambda e, _tt=tt, _w=w: e.tensor_tensor(
            H_[:, 0:_w], SILB[:, FCH * _tt:FCH * _tt + _w],
            ps[_tt][:, 0:_w], ALU.mult))
        hev[(ch, tt)] = vcv
        p.wait("scalar", "vc", vcv)
        emit_qdq(p, QS, H_[:, 0:w], HQB[tt % 4][:, 0:w], 128, w, "sc")
        if ch == NFCH - 1:
            p.op("vector", lambda e, _tt=tt, _w=w: e.tensor_copy(
                HQB[_tt % 4][:, _w - SEL:_w], H_[:, _w - SEL:_w]))
        qdq_done[(ch, tt)] = p.op("vector", lambda e: e.engine_nop())

    load_g(0)
    load_u(0)
    load_g(1)
    for ch in range(NFCH):
        for tt in range(8):
            g_block(ch, tt)
        if ch + 2 < NFCH:
            load_g(ch + 2)
        for tt in range(8):
            u_block(ch, tt)
        if ch + 1 < NFCH:
            load_u(ch + 1)
    for tt in range(5, 8):
        trans_block(NFCH - 1, tt)

    p.phase_barrier()

    att_ld = p.dma("gpsimd", ATT2F[:], attnt_d, "gat")

    evac_c1 = {}
    mm_c1 = {}
    wd_ld = {}
    wd_sem = ["wd", "we"]

    def c1_load_wdc(sl):
        par = sl % 2
        if sl >= 2:
            p.wait("sync", MM, mm_c1.get(sl - 2, 0))
        wd_ld[sl] = p.dma("sync", WDC[par][:],
                          wdt[128 * sl:128 * (sl + 1), :], wd_sem[par])

    c1_load_wdc(0)
    c1_load_wdc(1)
    hq_sems = ["hqa", "hqb", "hqc", "hqd"]
    hq_ld = {}
    hq_rd = {}

    def c1_load_hq(gi):
        b = gi % 4
        sl_, ft_ = divmod(gi, FHALF)
        fh_ = sl_ % 2
        prev = hq_rd.get(gi - 4)
        if prev:
            p.wait("sync", MM, prev)
        r0 = FHALF * 128 * fh_ + 128 * ft_
        hq_ld[gi] = p.dma("sync", HQF[b][:], hqt_d[r0:r0 + 128, :],
                          hq_sems[b])

    for pre in range(4):
        c1_load_hq(pre)
    ds_sems = ["dsa", "dsb"]
    ds_val = {}
    for dc in range(8):
        for fh in range(2):
            sl = dc * 2 + fh
            par = sl % 2
            for ft in range(FHALF):
                gi = sl * FHALF + ft
                b = gi % 4
                p.wait("tensor", hq_sems[b], hq_ld[gi])
                if ft == 0:
                    p.wait("tensor", wd_sem[par], wd_ld[sl])
                for tt in range(8):
                    if fh == 0 and ft == 0 and dc > 0:
                        p.wait("tensor", "vc", evac_c1[(dc - 1, tt)])
                    mmv = p.op("tensor",
                               lambda e, _ft=ft, _fh=fh, _b=b, _tt=tt,
                               _par=par: e.matmul(
                                   ps[_tt][:, 0:512],
                                   HQF[_b][:, 128 * _tt:128 * (_tt + 1)],
                                   WDC[_par][:, 512 * _ft:512 * (_ft + 1)],
                                   start=(_ft == 0 and _fh == 0),
                                   stop=(_ft == FHALF - 1 and _fh == 1)),
                               MM if tt == 7 else None)
                hq_rd[gi] = mmv
                mm_c1[sl] = mmv
                if gi + 4 < 16 * FHALF:
                    c1_load_hq(gi + 4)
            if sl + 2 < 16:
                c1_load_wdc(sl + 2)
        p.wait("vector", MM, mm_c1[dc * 2 + 1])
        for tt in range(8):
            dpar = tt % 2
            pv = ds_val.get((dc, tt - 2), ds_val.get((dc - 1, tt + 6)))
            if pv is not None:
                p.wait("vector", ds_sems[dpar], pv)
            vcv = p.op("vector", lambda e, _tt=tt, _dp=dpar: e.tensor_copy(
                DSTG[_dp][:], ps[_tt][:, 0:512]))
            evac_c1[(dc, tt)] = vcv
            p.wait("sync", "vc", vcv)
            ds_val[(dc, tt)] = p.dma(
                "sync", down_d[128 * tt:128 * (tt + 1),
                               512 * dc:512 * (dc + 1)],
                DSTG[dpar][:], ds_sems[dpar])

    p.phase_barrier()

    lr_ld = p.dma("gpsimd", LRB[0:64, :], lrb, "glr")
    ost_val = {}
    dt_sem = ["dtm", "dtn"]
    xt_sem = ["xt", "xu"]
    c2_ld = {}
    c2_mm = {}
    c2_ev = {}

    def c2_load(gi):
        dc, tt = divmod(gi, 8)
        par = gi % 2
        prev_ev = c2_ev.get(gi - 2)
        if prev_ev:
            p.wait("sync", "vc", prev_ev)
        d = p.dma("sync", DT2[par][:], down_d[128 * tt:128 * (tt + 1),
                                              512 * dc:512 * (dc + 1)],
                  dt_sem[par])
        x = p.dma("sync", XT2[par][:], xres[128 * tt:128 * (tt + 1),
                                            512 * dc:512 * (dc + 1)],
                  xt_sem[par])
        c2_ld[gi] = (d, x)

    gw_sem = ["gwa", "gwb"]
    for dc in range(8):
        b = dc % 2
        if dc >= 2:
            p.wait("gpsimd", MM, c2_mm[(dc - 2) * 8 + 7])
        gv = p.dma("gpsimd", WOC[b][:],
                   wo[128 * dc:128 * (dc + 1), :], gw_sem[b])
        if dc == 0:
            c2_load(0)
            c2_load(1)
        p.wait("tensor", gw_sem[b], gv)
        if dc == 0:
            p.wait("tensor", "glr", lr_ld)
            p.wait("tensor", "gat", att_ld)
        for tt in range(8):
            gi = dc * 8 + tt
            par = gi % 2
            d, x = c2_ld[gi]
            ev_prev = c2_ev.get(gi - 2)
            if ev_prev:
                p.wait("tensor", "vc", ev_prev)
            for dt in range(32):
                p.op("tensor", lambda e, _dt=dt, _b=b, _tt=tt, _par=par:
                     e.matmul(
                         ps[_par][:, 0:512],
                         ATT2F[:, TOK * _dt + 128 * _tt:
                               TOK * _dt + 128 * (_tt + 1)],
                         WOC[_b][:, 512 * _dt:512 * (_dt + 1)],
                         start=(_dt == 0), stop=False))
            mmv = p.op("tensor", lambda e, _b=b, _tt=tt, _dc=dc, _par=par: e.matmul(
                ps[_par][:, 0:512], O1S[0:64, 128 * _tt:128 * (_tt + 1)],
                LRB[0:64, 512 * _dc:512 * (_dc + 1)],
                start=False, stop=True), MM)
            c2_mm[gi] = mmv
            p.wait("vector", MM, mmv)
            p.wait("vector", dt_sem[par], d)
            p.wait("vector", xt_sem[par], x)
            stp = c2_ev.get(gi - 2)
            if stp is not None:
                p.wait("vector", "ost" if (gi - 2) % 2 == 0 else "osu",
                       ost_val[gi - 2])
            p.op("vector", lambda e, _par=par: e.tensor_tensor(
                OS2[_par][:], ps[_par][:, 0:512], DT2[_par][:], ALU.add))
            vcv = p.op("vector", lambda e, _par=par: e.tensor_tensor(
                OS2[_par][:], OS2[_par][:], XT2[_par][:], ALU.add))
            c2_ev[gi] = vcv
            p.wait("sync", "vc", vcv)
            ost_val[gi] = p.dma("sync", out[128 * tt:128 * (tt + 1),
                                            512 * dc:512 * (dc + 1)],
                                OS2[par][:], "ost" if gi % 2 == 0 else "osu")
            if gi + 2 < 64:
                c2_load(gi + 2)

    p.wait("sync", "ost", p.val("ost"))
    p.wait("sync", "osu", p.val("osu"))

    sems = {}
    stack = contextlib.ExitStack()
    for name in (LD, STs, "vc", "sc", MM, GP, "gpc", "xq", "wga", "wgb",
                 "wua", "wub",
                 "sta", "stb", "wd", "we", "hqa", "hqb", "hqc", "hqd",
                 "dsa", "dsb", "gat", "dtm",
                 "dtn", "xt", "xu", "ost", "osu", "glr", "gwa", "gwb"):
        sems[name] = stack.enter_context(nc.semaphore(name))
    with nc.Block() as block:
        p.replay(block, sems)
    stack.close()
    return nc


def _launch2_inputs(inputs, res1, PR):
    X = np.asarray(inputs["L0_attn_hidden_states"]).reshape(NB * S, D)
    up = np.asarray(inputs["L0_ffn_up_reorder_index"]).astype(np.int64)
    dn = np.asarray(inputs["L0_ffn_down_reorder_index"]).astype(np.int64)

    wg_full = np.concatenate([res1[c]["wg_o"] for c in range(NCORES)], axis=0)
    wu_full = np.concatenate([res1[c]["wu_o"] for c in range(NCORES)], axis=0)
    wd_full = np.concatenate([res1[c]["wd_o"] for c in range(NCORES)], axis=1)
    WGT = wg_full.T
    WUT = wu_full.T
    WDT = wd_full.T
    def _blk_gu(W):
        W2 = np.zeros((NFCH, 128, 32, FCH), dtype=BF)
        for ch in range(NFCH):
            w = CW[ch]
            blk = W[:, FCH * ch:FCH * ch + w].reshape(32, 128, w)
            W2[ch, :, :, :w] = blk.transpose(1, 0, 2)
        return np.ascontiguousarray(W2.reshape(NFCH * 128, 32 * FCH))

    WGT2 = _blk_gu(WGT)
    WUT2 = _blk_gu(WUT)
    WDT2 = np.ascontiguousarray(
        WDT.reshape(2, FHALF, 128, 8, 512).transpose(3, 0, 2, 1, 4)
        .reshape(16 * 128, FHALF * 512))

    ucst = (np.arange(128)[None, :] >= np.arange(128)[:, None]).astype(BF)
    one1 = np.ones((1, 128), dtype=BF)
    onec = np.ones((128, 1), dtype=BF)
    idn = np.eye(128, dtype=BF)
    wo = np.asarray(inputs["L0_attn_w_o"], np.float32)
    WO2 = np.ascontiguousarray(
        wo.reshape(32, 128, 8, 512).transpose(2, 1, 0, 3)
        .reshape(8 * 128, 32 * 512))
    lrbi = np.ascontiguousarray(np.asarray(inputs["L0_ffn_lr_B"], np.float32))

    in_maps = []
    for c in range(NCORES):
        b, par = c // 2, c % 2
        slots = SLOT_A if par == 0 else SLOT_B
        pair = [2 * b, 2 * b + 1]
        pr_pair = PR[TOK * pair[0]:TOK * (pair[0] + 1)].tolist() +                   PR[TOK * pair[1]:TOK * (pair[1] + 1)].tolist()
        rel = np.asarray(pr_pair, dtype=np.int64) - 2048 * b
        inv = np.empty(S, dtype=np.int64)
        inv[rel] = np.arange(S)
        kt_cat = np.concatenate([res1[pair[0]]["kt_o"],
                                 res1[pair[1]]["kt_o"]], axis=1)
        ktb = np.ascontiguousarray(kt_cat[:, inv])
        v_cat = np.concatenate([res1[pair[0]]["v_o"],
                                res1[pair[1]]["v_o"]], axis=0)
        v_nat = v_cat[inv, :]
        vaug = np.empty((S, NKV * 129), dtype=BF)
        for h in range(NKV):
            vaug[:, 129 * h:129 * h + 128] = v_nat[:, 128 * h:128 * (h + 1)]
            vaug[:, 129 * h + 128] = np.ones(S, dtype=BF)
        mskr = np.zeros((8, 2, 128, 128), dtype=np.float32)
        for si, t0 in enumerate(slots):
            E = EXT[si]
            for j in range(2):
                kt = E - 2 + j
                off = 128 * (t0 - kt)
                for q in range(128):
                    thr = q + off + 1
                    if thr <= 0:
                        mskr[si, j, 0, q] = -BIGM
                    elif thr <= 127:
                        mskr[si, j, thr, q] = -BIGM
        xqt2 = np.ascontiguousarray(
            res1[c]["xq_o"].T.reshape(32, 128, TOK).transpose(1, 0, 2)
            .reshape(128, 32 * TOK))
        in_maps.append({
            "xres": np.ascontiguousarray(X[PR[TOK * c:TOK * (c + 1)]]),
            "xqt": xqt2,
            "qt": res1[c]["qt_o"],
            "ktb": ktb,
            "vaug": vaug,
            "o1t": res1[c]["o1t_o"],
            "lrb": lrbi,
            "wo": WO2,
            "wgt": WGT2, "wut": WUT2, "wdt": WDT2,
            "mskr": mskr.astype(BF),
            "ucst": ucst, "one1": one1, "onec": onec, "idn": idn,
        })
    return in_maps


def kernel(**inputs):
    in_maps, PR = _launch1_inputs(inputs)
    nc1 = build_launch1()
    res1o = run_bass_kernel_spmd(nc1, in_maps, core_ids=list(range(NCORES)))
    res1 = res1o.results
    in_maps2 = _launch2_inputs(inputs, res1, PR)
    nc2 = build_launch2()
    res2 = run_bass_kernel_spmd(nc2, in_maps2, core_ids=list(range(NCORES)))
    out = np.empty((NB * S, D), dtype=np.float32)
    for c in range(NCORES):
        out[PR[TOK * c:TOK * (c + 1)]] = res2.results[c]["out"]
    return out.reshape(NB, S, D)
